# version 03
# baseline (speedup 1.0000x reference)
import numpy as np
import ml_dtypes
from contextlib import ExitStack

import concourse.bass as bass
import concourse.mybir as mybir
from concourse.bass_utils import run_bass_kernel_spmd

F32 = mybir.dt.float32
BF16 = mybir.dt.bfloat16
AF = mybir.ActivationFunctionType
ALU = mybir.AluOpType
AX = mybir.AxisListType
BFNP = ml_dtypes.bfloat16

NCORE = 8
B, T, D = 4, 2048, 2048
NTOK = B * T
TPC = T // NCORE
RPC = B * TPC
NEG = -1.0e5
DBG_CCH = 0

class Prog:

    ENGINES = ("sync", "gpsimd", "vector", "scalar", "tensor")
    EPOCH_LIMIT = 24000

    def __init__(self, nc):
        self.nc = nc
        self.groups = [[]]

    def op(self, eng, fn):
        self.groups[-1].append((eng, fn, 1, "op"))

    def dma(self, eng, out, in_):
        self.groups[-1].append((eng, (lambda e, o=out, i=in_:
                                      e.dma_start(out=o, in_=i)), 16, "op"))

    def cc(self, fn):
        self.bar()
        self.groups[-1].append(("gpsimd", fn, 1, "cc"))
        self.bar()

    def bar(self):
        if self.groups[-1]:
            self.groups.append([])

    def emit(self):
        self.bar()
        self.groups = [g for g in self.groups if g]
        epochs, ep, cum, ep_totals = [], 0, 0, []
        for g in self.groups:
            ginc = sum(i for (_, _, i, _) in g)
            if cum + ginc > self.EPOCH_LIMIT:
                ep_totals.append(cum)
                ep += 1
                cum = 0
            epochs.append((ep, cum))
            cum += ginc
        ep_totals.append(cum)
        nsem = ep + 1
        nc = self.nc
        with ExitStack() as es:
            sems = [es.enter_context(nc.semaphore("gsem%d" % i))
                    for i in range(nsem)]
            ccsem = es.enter_context(nc.semaphore("ccsem"))
            block = es.enter_context(nc.Block())
            cc_idx = [0]
            cc_of = {}
            for gi, g in enumerate(self.groups):
                for oi, o in enumerate(g):
                    if o[3] == "cc":
                        cc_idx[0] += 1
                        cc_of[(gi, oi)] = cc_idx[0]

            for name in self.ENGINES:
                has = any(any(o[0] == name for o in g) for g in self.groups)
                if not has and name != "sync":
                    continue

                def make(name=name):
                    def body(eng):
                        cur_ep = 0
                        for gi, g in enumerate(self.groups):
                            mine = [(oi, o) for oi, o in enumerate(g)
                                    if o[0] == name]
                            if not mine:
                                continue
                            gep, gstart = epochs[gi]
                            while cur_ep < gep:
                                eng.wait_ge(sems[cur_ep], ep_totals[cur_ep])
                                cur_ep += 1
                            if gstart > 0:
                                eng.wait_ge(sems[gep], gstart)
                            for (oi, (_, fn, inc, kind)) in mine:
                                if kind == "cc":
                                    r = fn(eng)
                                    r.then_inc(ccsem)
                                    eng.wait_ge(ccsem, cc_of[(gi, oi)])
                                    eng.nop().then_inc(sems[gep], 1)
                                else:
                                    r = fn(eng)
                                    r.then_inc(sems[gep], inc)
                        if name == "sync":
                            while cur_ep < nsem - 1:
                                eng.wait_ge(sems[cur_ep], ep_totals[cur_ep])
                                cur_ep += 1
                            eng.wait_ge(sems[nsem - 1], ep_totals[nsem - 1])
                    return body
                getattr(block, name)(make())


def emit_deltanet(nc, p, ext, NKH, KVR, gate_order, fused=False, dbg_t=None):
    NVH = NKH * KVR
    CT = 2 * NKH + NVH
    CW = CT * 128
    NZ = NVH * 128
    NBA = 2 * NVH
    NBLK = NTOK // 512
    dbg = dbg_t is not None

    x = ext.get("x")
    wmix = ext["wmix"]; wz = ext["wz"]; wba = ext["wba"]; convw = ext["convw"]
    dtb = ext["dtb"]; alog = ext["alog"]; normw = ext["normw"]
    ident = ext["ident"]; trii = ext["trii"]
    maskl = ext["maskl"]; masku = ext["masku"]; identv = ext["identv"]
    gatedT = ext["gout"]
    xlnT = ext["xlnT"]


    with ExitStack() as st:
        en = st.enter_context
        wmix_sb = en(nc.sbuf_tensor("wmix_sb", [128, 16 * CW], BF16))
        wz_sb = en(nc.sbuf_tensor("wz_sb", [128, 16 * NZ], BF16))
        wba_sb = en(nc.sbuf_tensor("wba_sb", [128, 16 * NBA], BF16))
        convw_sb = en(nc.sbuf_tensor("convw_sb", [128, CT * 4], F32))
        dtb_sb = en(nc.sbuf_tensor("dtb_sb", [128, NVH], F32))
        negA_sb = en(nc.sbuf_tensor("negA_sb", [128, NVH], F32))
        normw_sb = en(nc.sbuf_tensor("normw_sb", [128, 1], F32))
        idf_sb = en(nc.sbuf_tensor("idf_sb", [128, 128], F32))
        idb_sb = en(nc.sbuf_tensor("idb_sb", [128, 128], BF16))
        trii_sb = en(nc.sbuf_tensor("trii_sb", [128, 128], F32))
        maskl_sb = en(nc.sbuf_tensor("maskl_sb", [128, NVH * 128], F32))
        masku_sb = en(nc.sbuf_tensor("masku_sb", [128, NVH * 128], F32))
        identv_sb = en(nc.sbuf_tensor("identv_sb", [128, NVH * 128], F32))
        ones_rf = en(nc.sbuf_tensor("ones_rf", [1, 128], F32))
        ones_cf = en(nc.sbuf_tensor("ones_cf", [128, 1], F32))
        xt = en(nc.sbuf_tensor("xt", [128, 2 * D], F32))
        lnscr = en(nc.sbuf_tensor("lnscr", [128, D], F32))
        lnsq = en(nc.sbuf_tensor("lnsq", [128, D], F32))
        stat = en(nc.sbuf_tensor("stat", [128, 8], F32))
        xlnb = en(nc.sbuf_tensor("xlnb", [128, D], BF16))
        tstage = en(nc.sbuf_tensor("tstage", [128, 16 * 128], BF16))
        rhs_sb = en(nc.sbuf_tensor("rhs_sb", [128, 2 * 16 * 512], BF16))
        mixT = en(nc.sbuf_tensor("mixT", [128, CT * 515], BF16))
        convT = en(nc.sbuf_tensor("convT", [128, CT * 512], BF16))
        szT = en(nc.sbuf_tensor("szT", [128, NVH * 512], BF16))
        qkn = en(nc.sbuf_tensor("qkn", [128, 2 * NKH * 512], BF16))
        ba_sb = en(nc.sbuf_tensor("ba_sb", [128, 4 * NBA], F32))
        stsc = en(nc.sbuf_tensor("stsc", [128, 8 * NVH], F32))
        beta_c = en(nc.sbuf_tensor("beta_c", [128, NVH], F32))
        g_c = en(nc.sbuf_tensor("g_c", [128, NVH], F32))
        cg_c = en(nc.sbuf_tensor("cg_c", [128, NVH], F32))
        gam_c = en(nc.sbuf_tensor("gam_c", [128, NVH], F32))
        ghat_c = en(nc.sbuf_tensor("ghat_c", [128, NVH], F32))
        gamC_c = en(nc.sbuf_tensor("gamC_c", [128, NVH], F32))
        rows = en(nc.sbuf_tensor("rows", [1, 2 * NVH * 128], F32))
        nrow = en(nc.sbuf_tensor("nrow", [1, NVH * 128], F32))
        grow = en(nc.sbuf_tensor("grow", [1, NVH * 128], F32))
        E_sb = en(nc.sbuf_tensor("E_sb", [128, NVH * 128], F32))
        Ew = en(nc.sbuf_tensor("Ew", [128, 2 * NVH * 128], F32))
        Dl_sb = en(nc.sbuf_tensor("Dl_sb", [128, NVH * 128], F32))
        Dui_sb = en(nc.sbuf_tensor("Dui_sb", [128, NVH * 128], F32))
        Dus_sb = en(nc.sbuf_tensor("Dus_sb", [128, NVH * 128], F32))
        lnr = en(nc.sbuf_tensor("lnr", [1, 3 * 512], F32))
        M0 = en(nc.sbuf_tensor("M0", [128, 128], BF16))
        M1 = en(nc.sbuf_tensor("M1", [128, 128], BF16))
        MTs = en(nc.sbuf_tensor("MTs", [128, 7 * 128], BF16))
        X0 = en(nc.sbuf_tensor("X0", [128, 128], F32))
        X1 = en(nc.sbuf_tensor("X1", [128, 128], F32))
        Xbf = en(nc.sbuf_tensor("Xbf", [128, 128], BF16))
        Ubf = en(nc.sbuf_tensor("Ubf", [128, 128], BF16))
        attnT = en(nc.sbuf_tensor("attnT", [128, 128], BF16))
        qbar = en(nc.sbuf_tensor("qbar", [128, 128], BF16))
        khat = en(nc.sbuf_tensor("khat", [128, 128], BF16))
        scrA = en(nc.sbuf_tensor("scrA", [128, 512], F32))
        scrB = en(nc.sbuf_tensor("scrB", [128, 512], F32))
        scrC = en(nc.sbuf_tensor("scrC", [128, 512], F32))
        scrD = en(nc.sbuf_tensor("scrD", [128, 512], F32))
        Sst = en(nc.sbuf_tensor("Sst", [128, 2 * NZ], F32))
        Sbf = en(nc.sbuf_tensor("Sbf", [128, NZ], BF16))
        gst = en(nc.sbuf_tensor("gst", [128, NVH * 512], BF16))
        pj0 = en(nc.psum_tensor("pj0", [128, 512], F32))
        pj1 = en(nc.psum_tensor("pj1", [128, 512], F32))
        pA = en(nc.psum_tensor("pA", [128, 512], F32))
        pKQ = en(nc.psum_tensor("pKQ", [128, 512], F32))
        pE = en(nc.psum_tensor("pE", [128, 512], F32))
        pX = en(nc.psum_tensor("pX", [128, 512], F32))
        pT = en(nc.psum_tensor("pT", [128, 1024], BF16))
        pRow = en(nc.psum_tensor("pRow", [128, 512], F32))

        def dmp(name, ap, cast=False):
            if not dbg:
                return
            p.bar()
            p.dma("gpsimd" if cast else "sync", dbg_t[name], ap)
            p.bar()

        p.dma("gpsimd", wmix_sb[:].rearrange("p (dt c) -> p dt c", c=CW),
              wmix.rearrange("(dt p) c -> p dt c", p=128))
        p.dma("gpsimd", wz_sb[:].rearrange("p (dt c) -> p dt c", c=NZ),
              wz.rearrange("(dt p) c -> p dt c", p=128))
        p.dma("gpsimd", wba_sb[:].rearrange("p (dt c) -> p dt c", c=NBA),
              wba.rearrange("(dt p) c -> p dt c", p=128))
        p.dma("sync", convw_sb[:], convw)
        p.dma("sync", dtb_sb[:], dtb)
        p.dma("sync", negA_sb[:], alog)
        p.dma("sync", normw_sb[:], normw)
        p.dma("sync", idf_sb[:], ident)
        p.dma("gpsimd", idb_sb[:], ident)
        p.dma("sync", trii_sb[:], trii)
        p.dma("sync", maskl_sb[:], maskl)
        p.dma("sync", masku_sb[:], masku)
        p.dma("sync", identv_sb[:], identv)
        p.op("vector", lambda e: e.memset(ones_rf[:], 1.0))
        p.op("vector", lambda e: e.memset(ones_cf[:], 1.0))
        p.bar()
        p.op("scalar", lambda e: e.activation(negA_sb[:], negA_sb[:], AF.Exp))
        p.bar()
        p.op("vector", lambda e: e.tensor_scalar(negA_sb[:], negA_sb[:], -1.0,
                                                 None, ALU.mult))
        p.bar()

        for i in ([] if fused else range(64)):
            if i == 0:
                p.dma("sync", xt[:, 0:D], x[0:128, :])
                p.bar()
            par = i % 2
            xv = xt[:, par * D:(par + 1) * D]
            if i + 1 < 64:
                p.dma("sync", xt[:, (1 - par) * D:(2 - par) * D],
                      x[(i + 1) * 128:(i + 2) * 128, :])
            p.op("vector", lambda e, xv=xv: e.reduce_sum(stat[:, 0:1], xv, axis=AX.X))
            p.bar()
            p.op("vector", lambda e: e.tensor_scalar(stat[:, 1:2], stat[:, 0:1],
                                                     -1.0 / D, None, ALU.mult))
            p.bar()
            p.op("vector", lambda e, xv=xv: e.tensor_scalar(lnscr[:], xv,
                                                            stat[:, 1:2], None, ALU.add))
            p.bar()
            p.op("vector", lambda e: e.tensor_tensor(lnsq[:], lnscr[:], lnscr[:],
                                                     ALU.mult))
            p.bar()
            p.op("vector", lambda e: e.reduce_sum(stat[:, 6:7], lnsq[:], axis=AX.X))
            p.bar()
            p.op("vector", lambda e: e.tensor_scalar(stat[:, 6:7], stat[:, 6:7],
                                                     1.0 / D, 1e-5, ALU.mult, ALU.add))
            p.bar()
            p.op("scalar", lambda e: e.activation(stat[:, 7:8], stat[:, 6:7], AF.Sqrt))
            p.bar()
            p.op("vector", lambda e: e.reciprocal(stat[:, 6:7], stat[:, 7:8]))
            p.bar()
            p.op("vector", lambda e: e.tensor_scalar(xlnb[:], lnscr[:],
                                                     stat[:, 6:7], None, ALU.mult))
            p.bar()
            for half in range(2):
                for k in range(8):
                    blk = half * 8 + k
                    p.op("tensor", lambda e, blk=blk, half=half:
                         e.transpose(pT[:, (blk - half * 8) * 128:
                                        (blk - half * 8 + 1) * 128],
                                     xlnb[:, blk * 128:(blk + 1) * 128], idb_sb[:]))
                p.bar()
                p.op("vector", lambda e, half=half:
                     e.tensor_copy(tstage[:, half * 1024:(half + 1) * 1024],
                                   pT[:, 0:1024]))
                p.bar()
            p.dma("sync",
                  xlnT[:, i * 128:(i + 1) * 128].rearrange(
                      "(blk q) c -> q blk c", q=128),
                  tstage[:].rearrange("q (blk c) -> q blk c", c=128))
            p.bar()

        NDT = 16
        Svalid = [0]

        def rview(buf, dt, lo, hi):
            return rhs_sb[:, (buf * 16 + dt) * 512 + lo:(buf * 16 + dt) * 512 + hi]

        def load_rhs(tb, buf):
            dv = rhs_sb[:, buf * 16 * 512:(buf + 1) * 16 * 512].rearrange(
                "p (dt c) -> p dt c", c=512)
            if not fused:
                p.dma("sync", dv,
                      xlnT[:, tb * 512:(tb + 1) * 512]
                      .rearrange("(dt p) c -> p dt c", p=128))
            else:
                b = tb // 4
                r0 = 2 * (tb % 4)
                for h2 in range(2):
                    blk = r0 + h2
                    p.dma("sync", dv[:, :, h2 * 256:(h2 + 1) * 256],
                          xlnT[blk * D:(blk + 1) * D, b * 256:(b + 1) * 256]
                          .rearrange("(dt p) c -> p dt c", p=128))

        load_rhs(0, 0)
        p.bar()
        dmp("d_xlnT", rhs_sb[:, 0:16 * 512], cast=True)

        for tb in range(NBLK):
            buf = tb % 2
            bstart = (tb % 4 == 0)
            if tb + 1 < NBLK:
                load_rhs(tb + 1, 1 - buf)
            for ct in range(CT):
                bank = (pj0, pj1)[ct % 2]
                for dt in range(NDT):
                    rv = rview(buf, dt, 0, 512)
                    p.op("tensor", lambda e, ct=ct, dt=dt, bank=bank, rv=rv:
                         e.matmul(bank[:], wmix_sb[:, dt * CW + ct * 128:
                                                   dt * CW + (ct + 1) * 128],
                                  rv, start=(dt == 0), stop=(dt == NDT - 1)))
                p.bar()
                p.op("vector", lambda e, ct=ct, bank=bank:
                     e.tensor_copy(mixT[:, ct * 515 + 3:ct * 515 + 515], bank[:]))
                p.bar()
            for zt in range(NVH):
                bank = (pj0, pj1)[zt % 2]
                for dt in range(NDT):
                    rv = rview(buf, dt, 0, 512)
                    p.op("tensor", lambda e, zt=zt, dt=dt, bank=bank, rv=rv:
                         e.matmul(bank[:], wz_sb[:, dt * NZ + zt * 128:
                                                 dt * NZ + (zt + 1) * 128],
                                  rv, start=(dt == 0), stop=(dt == NDT - 1)))
                p.bar()
                p.op("scalar", lambda e, zt=zt, bank=bank:
                     e.activation(szT[:, zt * 512:(zt + 1) * 512], bank[:], AF.Silu))
                p.bar()
            for cch in range(4):
                for dt in range(NDT):
                    rv = rview(buf, dt, cch * 128, (cch + 1) * 128)
                    p.op("tensor", lambda e, cch=cch, dt=dt, rv=rv:
                         e.matmul(pA[:, 256 + cch * NBA:256 + (cch + 1) * NBA],
                                  rv, wba_sb[:, dt * NBA:(dt + 1) * NBA],
                                  start=(dt == 0), stop=(dt == NDT - 1)))
            p.bar()
            p.op("vector", lambda e: e.tensor_copy(ba_sb[:], pA[:, 256:256 + 4 * NBA]))
            p.bar()
            if tb == 0:
                dmp("d_ba", ba_sb[:])
            if bstart:
                for ct in range(CT):
                    p.op("vector", lambda e, ct=ct:
                         e.memset(mixT[:, ct * 515:ct * 515 + 3], 0.0))
                p.bar()
            for ct in range(CT):
                for s in range(4):
                    if s == 0:
                        p.op("vector", lambda e, ct=ct:
                             e.tensor_scalar(scrA[:], mixT[:, ct * 515:ct * 515 + 512],
                                             convw_sb[:, ct * 4:ct * 4 + 1],
                                             None, ALU.mult))
                    else:
                        p.op("vector", lambda e, ct=ct, s=s:
                             e.scalar_tensor_tensor(
                                 scrA[:], mixT[:, ct * 515 + s:ct * 515 + s + 512],
                                 convw_sb[:, ct * 4 + s:ct * 4 + s + 1],
                                 scrA[:], ALU.mult, ALU.add))
                p.bar()
                p.op("scalar", lambda e, ct=ct:
                     e.activation(convT[:, ct * 512:(ct + 1) * 512], scrA[:], AF.Silu))
                p.bar()
            if tb % 4 != 3:
                for ct in range(CT):
                    p.op("vector", lambda e, ct=ct:
                         e.tensor_copy(mixT[:, ct * 515:ct * 515 + 3],
                                       mixT[:, ct * 515 + 512:ct * 515 + 515]))
                p.bar()
            if tb == 0:
                dmp("d_convT", convT[:], cast=True)
                dmp("d_szT", szT[:], cast=True)
            for kh in range(NKH):
                for qk in range(2):
                    ct = kh if qk == 0 else NKH + kh
                    p.op("scalar", lambda e, ct=ct:
                         e.activation(scrA[:], convT[:, ct * 512:(ct + 1) * 512],
                                      AF.Square))
                    p.bar()
                    p.op("tensor", lambda e:
                         e.matmul(pRow[0:1, 0:512], ones_cf[:], scrA[:],
                                  start=True, stop=True))
                    p.bar()
                    p.op("vector", lambda e: e.tensor_scalar(
                        lnr[0:1, 0:512], pRow[0:1, 0:512], 1.0, 1e-6,
                        ALU.mult, ALU.add))
                    p.bar()
                    p.op("vector", lambda e: e.reciprocal(lnr[0:1, 512:1024],
                                                          lnr[0:1, 0:512]))
                    p.bar()
                    sc = (1.0 / 128.0) if qk == 0 else 1.0
                    p.op("scalar", lambda e, sc=sc: e.activation(
                        lnr[0:1, 1024:1536], lnr[0:1, 512:1024], AF.Sqrt, scale=sc))
                    p.bar()
                    p.op("tensor", lambda e:
                         e.matmul(pE[:], ones_rf[:], lnr[0:1, 1024:1536],
                                  start=True, stop=True))
                    p.bar()
                    p.op("vector", lambda e, ct=ct, kh=kh, qk=qk:
                         e.tensor_tensor(qkn[:, (qk * NKH + kh) * 512:
                                              (qk * NKH + kh + 1) * 512],
                                         convT[:, ct * 512:(ct + 1) * 512],
                                         pE[:], ALU.mult))
                    p.bar()
            if tb == 0:
                dmp("d_qkn", qkn[:], cast=True)
            if bstart:
                p.op("vector", lambda e: e.memset(Sst[:], 0.0))
                p.op("vector", lambda e: e.memset(Sbf[:], 0.0))
                p.bar()
                Svalid[0] = 0

            for cch in range(4):
                tl = cch * 128
                par = Svalid[0]
                bcol = ba_sb[:, cch * NBA:cch * NBA + NVH]
                acol = ba_sb[:, cch * NBA + NVH:(cch + 1) * NBA]
                p.op("vector", lambda e, acol=acol:
                     e.tensor_tensor(stsc[:, 0:NVH], acol, dtb_sb[:], ALU.add))
                p.bar()
                p.op("vector", lambda e:
                     e.tensor_scalar(stsc[:, NVH:2 * NVH], stsc[:, 0:NVH],
                                     -1.0, None, ALU.mult))
                p.op("vector", lambda e:
                     e.tensor_relu(stsc[:, 3 * NVH:4 * NVH], stsc[:, 0:NVH]))
                p.bar()
                p.op("vector", lambda e:
                     e.tensor_tensor(stsc[:, 2 * NVH:3 * NVH], stsc[:, 0:NVH],
                                     stsc[:, NVH:2 * NVH], ALU.min))
                p.bar()
                p.op("scalar", lambda e:
                     e.activation(stsc[:, 4 * NVH:5 * NVH],
                                  stsc[:, 2 * NVH:3 * NVH], AF.Exp))
                p.op("scalar", lambda e, bcol=bcol:
                     e.activation(beta_c[:], bcol, AF.Sigmoid))
                p.bar()
                p.op("scalar", lambda e:
                     e.activation(stsc[:, 5 * NVH:6 * NVH],
                                  stsc[:, 4 * NVH:5 * NVH], AF.Ln, bias=1.0))
                p.bar()
                p.op("vector", lambda e:
                     e.tensor_tensor(stsc[:, 6 * NVH:7 * NVH],
                                     stsc[:, 3 * NVH:4 * NVH],
                                     stsc[:, 5 * NVH:6 * NVH], ALU.add))
                p.bar()
                p.op("vector", lambda e:
                     e.tensor_tensor(g_c[:], negA_sb[:],
                                     stsc[:, 6 * NVH:7 * NVH], ALU.mult))
                p.bar()
                p.op("tensor", lambda e:
                     e.matmul(pKQ[:, 256:256 + NVH], trii_sb[:], g_c[:],
                              start=True, stop=True))
                p.bar()
                p.op("vector", lambda e: e.tensor_copy(cg_c[:], pKQ[:, 256:256 + NVH]))
                p.bar()
                p.op("scalar", lambda e: e.activation(gam_c[:], cg_c[:], AF.Exp))
                for j in range(NVH):
                    p.op("tensor", lambda e, j=j:
                         e.transpose(pRow[0:1, j * 128:(j + 1) * 128],
                                     cg_c[:, j:j + 1], idf_sb[:]))
                p.bar()
                p.op("vector", lambda e:
                     e.tensor_copy(rows[0:1, 0:NVH * 128], pRow[0:1, 0:NVH * 128]))
                p.bar()
                for j in range(NVH):
                    p.op("tensor", lambda e, j=j:
                         e.transpose(pRow[0:1, j * 128:(j + 1) * 128],
                                     beta_c[:, j:j + 1], idf_sb[:]))
                p.bar()
                p.op("vector", lambda e:
                     e.tensor_copy(rows[0:1, NVH * 128:2 * NVH * 128],
                                   pRow[0:1, 0:NVH * 128]))
                p.bar()
                p.op("scalar", lambda e:
                     e.activation(nrow[0:1, :], rows[0:1, 0:NVH * 128],
                                  AF.Copy, scale=-1.0))
                p.op("scalar", lambda e:
                     e.activation(grow[0:1, :], rows[0:1, 0:NVH * 128], AF.Exp))
                p.bar()
                p.op("tensor", lambda e:
                     e.matmul(pE[:, 0:NVH * 128], ones_rf[:], nrow[0:1, :],
                              start=True, stop=True))
                p.bar()
                p.op("vector", lambda e:
                     e.tensor_tensor(E_sb[:].rearrange("p (j c) -> p j c", c=128),
                                     pE[:, 0:NVH * 128]
                                     .rearrange("p (j c) -> p j c", c=128),
                                     cg_c[:].unsqueeze(-1)
                                     .broadcast_to((128, NVH, 128)), ALU.add))
                p.bar()
                p.op("vector", lambda e:
                     e.tensor_tensor(Ew[:, 0:NVH * 128], E_sb[:], maskl_sb[:],
                                     ALU.add))
                p.op("vector", lambda e:
                     e.scalar_tensor_tensor(Ew[:, NVH * 128:2 * NVH * 128],
                                            E_sb[:], -1.0, masku_sb[:],
                                            ALU.mult, ALU.add))
                p.bar()
                p.op("scalar", lambda e:
                     e.activation(Dl_sb[:], Ew[:, 0:NVH * 128], AF.Exp))
                p.op("scalar", lambda e:
                     e.activation(Dui_sb[:], Ew[:, NVH * 128:2 * NVH * 128], AF.Exp))
                p.bar()
                p.op("vector", lambda e:
                     e.tensor_tensor(Dus_sb[:], Dui_sb[:], identv_sb[:],
                                     ALU.subtract))
                p.op("vector", lambda e:
                     e.tensor_copy(ghat_c[:].unsqueeze(-1),
                                   Dui_sb[:].rearrange("p (j c) -> p j c", c=128)
                                   [:, :, 127:128]))
                p.bar()
                p.op("vector", lambda e:
                     e.tensor_tensor(gamC_c[:], gam_c[:], ghat_c[:], ALU.mult))
                p.bar()
                if tb == 0 and cch == DBG_CCH:
                    for nm, ap in [("d_cg", cg_c[:]), ("d_gam", gam_c[:]),
                                   ("d_ghat", ghat_c[:]), ("d_gamC", gamC_c[:]),
                                   ("d_beta", beta_c[:]), ("d_g", g_c[:]),
                                   ("d_rows", rows[:]), ("d_E", E_sb[:]),
                                   ("d_Dl", Dl_sb[:]), ("d_Dui", Dui_sb[:]),
                                   ("d_Dus", Dus_sb[:])]:
                        dmp(nm, ap)
                for kh in range(NKH):
                    kt = qkn[:, (NKH + kh) * 512 + tl:(NKH + kh) * 512 + tl + 128]
                    qt = qkn[:, kh * 512 + tl:kh * 512 + tl + 128]
                    p.op("tensor", lambda e, kt=kt, kh=kh:
                         e.matmul(pA[:, kh * 128:(kh + 1) * 128], kt, kt,
                                  start=True, stop=True))
                    p.op("tensor", lambda e, kt=kt, qt=qt, kh=kh:
                         e.matmul(pKQ[:, kh * 128:(kh + 1) * 128], kt, qt,
                                  start=True, stop=True))
                p.bar()
                for j in range(NVH):
                    kh = j // KVR
                    jsl = slice(j * 128, (j + 1) * 128)
                    kt = qkn[:, (NKH + kh) * 512 + tl:(NKH + kh) * 512 + tl + 128]
                    vt = convT[:, (2 * NKH + j) * 512 + tl:
                               (2 * NKH + j) * 512 + tl + 128]
                    pAj = pA[:, kh * 128:(kh + 1) * 128]
                    pKQj = pKQ[:, kh * 128:(kh + 1) * 128]
                    p.op("vector", lambda e, pAj=pAj, j=j, jsl=jsl:
                         e.scalar_tensor_tensor(M0[:], pAj, beta_c[:, j:j + 1],
                                                Dl_sb[:, jsl], ALU.mult, ALU.mult))
                    p.op("vector", lambda e, pAj=pAj, jsl=jsl:
                         e.tensor_tensor(scrB[:, 0:128], pAj, Dus_sb[:, jsl],
                                         ALU.mult))
                    p.op("tensor", lambda e, j=j:
                         e.matmul(pE[:, 0:128], ones_rf[:],
                                  rows[0:1, (NVH + j) * 128:(NVH + j + 1) * 128],
                                  start=True, stop=True))
                    p.op("tensor", lambda e, kt=kt, j=j, par=par:
                         e.matmul(pX[:, 0:128], kt, Sbf[:, j * 128:(j + 1) * 128],
                                  start=True, stop=True))
                    p.bar()
                    p.op("vector", lambda e:
                         e.tensor_tensor(M1[:], scrB[:, 0:128], pE[:, 0:128],
                                         ALU.mult))
                    p.op("tensor", lambda e, vt=vt:
                         e.transpose(pT[:, 0:128], vt, idb_sb[:]))
                    p.bar()
                    p.op("vector", lambda e: e.tensor_copy(MTs[:, 0:128], M1[:]))
                    p.op("vector", lambda e: e.tensor_copy(scrC[:, 0:128],
                                                           pT[:, 0:128]))
                    p.bar()
                    if tb == 0 and cch == DBG_CCH and j == 0:
                        p.op("vector", lambda e: e.tensor_copy(lnsq[:, 0:128],
                                                               pX[:, 0:128]))
                        p.bar()
                        dmp("d_P0", lnsq[:, 0:128])
                        dmp("d_Sin", Sbf[:], cast=True)
                        dmp("d_M", M0[:], cast=True)
                        dmp("d_MT", MTs[:, 0:128], cast=True)
                    p.op("vector", lambda e, j=j:
                         e.scalar_tensor_tensor(scrD[:, 0:128], pX[:, 0:128],
                                                gam_c[:, j:j + 1], scrC[:, 0:128],
                                                ALU.mult, ALU.subtract))
                    p.bar()
                    p.op("vector", lambda e, j=j:
                         e.tensor_scalar(X0[:], scrD[:, 0:128],
                                         beta_c[:, j:j + 1], -1.0,
                                         ALU.mult, ALU.mult))
                    p.bar()
                    p.op("vector", lambda e: e.tensor_copy(Xbf[:], X0[:]))
                    p.bar()
                    if tb == 0 and cch == DBG_CCH and j == 0:
                        dmp("d_B", X0[:])
                    Mcur = [M0, M1]
                    for lev in range(1, 7):
                        a, b2 = Mcur[(lev + 1) % 2], Mcur[lev % 2]
                        p.op("tensor", lambda e, lev=lev, a=a:
                             e.matmul(pX[:, 128:256],
                                      MTs[:, (lev - 1) * 128:lev * 128], a[:],
                                      start=True, stop=True))
                        p.op("tensor", lambda e, lev=lev, a=a:
                             e.matmul(pX[:, 256:384], a[:],
                                      MTs[:, (lev - 1) * 128:lev * 128],
                                      start=True, stop=True))
                        p.bar()
                        p.op("vector", lambda e, b2=b2:
                             e.tensor_copy(b2[:], pX[:, 128:256]))
                        p.op("vector", lambda e, lev=lev:
                             e.tensor_copy(MTs[:, lev * 128:(lev + 1) * 128],
                                           pX[:, 256:384]))
                        p.bar()
                    Xf = [X0, X1]
                    xp = 0
                    for lev in range(6, 0, -1):
                        p.op("tensor", lambda e, lev=lev:
                             e.matmul(pX[:, 128:256],
                                      MTs[:, lev * 128:(lev + 1) * 128], Xbf[:],
                                      start=True, stop=True))
                        p.bar()
                        p.op("vector", lambda e, xp=xp:
                             e.tensor_tensor(Xf[1 - xp][:], Xf[xp][:],
                                             pX[:, 128:256], ALU.add))
                        p.bar()
                        p.op("vector", lambda e, xp=xp:
                             e.tensor_copy(Xbf[:], Xf[1 - xp][:]))
                        p.bar()
                        xp = 1 - xp
                    p.op("tensor", lambda e:
                         e.matmul(pX[:, 128:256], MTs[:, 0:128], Xbf[:],
                                  start=True, stop=True))
                    p.bar()
                    p.op("vector", lambda e, xp=xp:
                         e.tensor_tensor(Xf[1 - xp][:], Xf[xp][:],
                                         pX[:, 128:256], ALU.subtract))
                    p.bar()
                    p.op("vector", lambda e, xp=xp:
                         e.tensor_copy(Ubf[:], Xf[1 - xp][:]))
                    p.bar()
                    if tb == 0 and cch == DBG_CCH and j == 0:
                        dmp("d_U", Ubf[:], cast=True)
                    p.op("vector", lambda e, pKQj=pKQj, jsl=jsl:
                         e.tensor_tensor(attnT[:], pKQj, Dui_sb[:, jsl], ALU.mult))
                    p.op("tensor", lambda e, j=j:
                         e.matmul(pE[:, 0:128], ones_rf[:],
                                  grow[0:1, j * 128:(j + 1) * 128],
                                  start=True, stop=True))
                    p.bar()
                    qt = qkn[:, kh * 512 + tl:kh * 512 + tl + 128]
                    p.op("vector", lambda e, qt=qt:
                         e.tensor_tensor(qbar[:], qt, pE[:, 0:128], ALU.mult))
                    p.bar()
                    if tb == 0 and cch == DBG_CCH and j == 0:
                        dmp("d_attnT", attnT[:], cast=True)
                        dmp("d_qbar", qbar[:], cast=True)
                    p.op("tensor", lambda e, j=j:
                         e.matmul(pX[:, 384:512], Sbf[:, j * 128:(j + 1) * 128],
                                  qbar[:], start=True, stop=False))
                    p.op("tensor", lambda e:
                         e.matmul(pX[:, 384:512], Ubf[:], attnT[:],
                                  start=False, stop=True))
                    p.bar()
                    if tb == 0 and cch == DBG_CCH and j == 0:
                        p.op("vector", lambda e: e.tensor_copy(scrA[:, 0:128],
                                                               pX[:, 384:512]))
                        p.bar()
                        dmp("d_oT", scrA[:, 0:128])
                    zt = szT[:, j * 512 + tl:j * 512 + tl + 128]
                    if gate_order == "mul_first":
                        p.op("vector", lambda e, zt=zt:
                             e.tensor_tensor(scrC[:, 0:128], pX[:, 384:512], zt,
                                             ALU.mult))
                        p.bar()
                        p.op("scalar", lambda e:
                             e.activation(scrD[:, 0:128], scrC[:, 0:128], AF.Square))
                        p.bar()
                        p.op("tensor", lambda e:
                             e.matmul(pRow[0:1, 0:128], ones_cf[:],
                                      scrD[:, 0:128], start=True, stop=True))
                        p.bar()
                        p.op("vector", lambda e:
                             e.tensor_scalar(lnr[0:1, 0:128], pRow[0:1, 0:128],
                                             1.0 / 128, 1e-6, ALU.mult, ALU.add))
                        p.bar()
                        p.op("vector", lambda e:
                             e.reciprocal(lnr[0:1, 128:256], lnr[0:1, 0:128]))
                        p.bar()
                        p.op("scalar", lambda e:
                             e.activation(lnr[0:1, 256:384], lnr[0:1, 128:256],
                                          AF.Sqrt))
                        p.bar()
                        p.op("tensor", lambda e:
                             e.matmul(pE[:, 0:128], ones_rf[:], lnr[0:1, 256:384],
                                      start=True, stop=True))
                        p.bar()
                        gv = gst[:, j * 512 + tl:j * 512 + tl + 128]
                        p.op("vector", lambda e, gv=gv:
                             e.scalar_tensor_tensor(
                                 gv, scrC[:, 0:128], normw_sb[:, 0:1],
                                 pE[:, 0:128], ALU.mult, ALU.mult))
                        p.bar()
                    else:
                        p.op("scalar", lambda e:
                             e.activation(scrD[:, 0:128], pX[:, 384:512], AF.Square))
                        p.bar()
                        p.op("tensor", lambda e:
                             e.matmul(pRow[0:1, 0:128], ones_cf[:],
                                      scrD[:, 0:128], start=True, stop=True))
                        p.bar()
                        p.op("vector", lambda e:
                             e.tensor_scalar(lnr[0:1, 0:128], pRow[0:1, 0:128],
                                             1.0 / 128, 1e-6, ALU.mult, ALU.add))
                        p.bar()
                        p.op("vector", lambda e:
                             e.reciprocal(lnr[0:1, 128:256], lnr[0:1, 0:128]))
                        p.bar()
                        p.op("scalar", lambda e:
                             e.activation(lnr[0:1, 256:384], lnr[0:1, 128:256],
                                          AF.Sqrt))
                        p.bar()
                        p.op("tensor", lambda e:
                             e.matmul(pE[:, 0:128], ones_rf[:], lnr[0:1, 256:384],
                                      start=True, stop=True))
                        p.bar()
                        p.op("vector", lambda e:
                             e.tensor_copy(scrB[:, 128:256], pX[:, 384:512]))
                        p.bar()
                        p.op("vector", lambda e:
                             e.scalar_tensor_tensor(scrC[:, 0:128], scrB[:, 128:256],
                                                    normw_sb[:, 0:1], pE[:, 0:128],
                                                    ALU.mult, ALU.mult))
                        p.bar()
                        gv = gst[:, j * 512 + tl:j * 512 + tl + 128]
                        p.op("vector", lambda e, gv=gv, zt=zt:
                             e.tensor_tensor(gv, scrC[:, 0:128], zt, ALU.mult))
                        p.bar()
                    p.op("tensor", lambda e, kt=kt:
                         e.transpose(pT[:, 128:256], kt, idb_sb[:]))
                    p.bar()
                    p.op("vector", lambda e, j=j:
                         e.tensor_scalar(khat[:], pT[:, 128:256],
                                         ghat_c[:, j:j + 1], None, ALU.mult))
                    p.bar()
                    if tb == 0 and cch == DBG_CCH and j == 0:
                        dmp("d_khat", khat[:], cast=True)
                    p.op("tensor", lambda e:
                         e.matmul(pX[:, 256:384], khat[:], Ubf[:],
                                  start=True, stop=True))
                    p.bar()
                    p.op("vector", lambda e, j=j, par=par:
                         e.scalar_tensor_tensor(
                             Sst[:, (1 - par) * NZ + j * 128:
                                 (1 - par) * NZ + (j + 1) * 128],
                             Sst[:, par * NZ + j * 128:par * NZ + (j + 1) * 128],
                             gamC_c[:, j:j + 1], pX[:, 256:384],
                             ALU.mult, ALU.add))
                    p.bar()
                    p.op("vector", lambda e, j=j, par=par:
                         e.tensor_copy(Sbf[:, j * 128:(j + 1) * 128],
                                       Sst[:, (1 - par) * NZ + j * 128:
                                           (1 - par) * NZ + (j + 1) * 128]))
                    p.bar()
                if tb == 0 and cch == DBG_CCH:
                    dmp("d_S", Sst[:, (1 - par) * NZ:(2 - par) * NZ])
                Svalid[0] = 1 - par
            if not fused:
                p.dma("sync",
                      gatedT[:, tb * 512:(tb + 1) * 512].rearrange(
                          "(j q) c -> q j c", q=128),
                      gst[:].rearrange("q (j c) -> q j c", c=512))
            else:
                b = tb // 4
                r0 = 2 * (tb % 4)
                gv = gst[:].rearrange("q (j c) -> q j c", c=512)
                for h2 in range(2):
                    blk = r0 + h2
                    p.dma("sync",
                          gatedT[blk * NZ:(blk + 1) * NZ, b * 256:(b + 1) * 256]
                          .rearrange("(j q) c -> q j c", q=128),
                          gv[:, :, h2 * 256:(h2 + 1) * 256])
            p.bar()


def build_deltanet(NKH, KVR, gate_order, dbg=False):
    NVH = NKH * KVR
    NZ = NVH * 128
    NBA = 2 * NVH
    CT = 2 * NKH + NVH
    nc = bass.Bass("TRN2", target_bir_lowering=False)
    ext = {}
    for nm, shp in [("x", [NTOK, D]), ("wmix", [D, CT * 128]), ("wz", [D, NZ]),
                    ("wba", [D, NBA]), ("convw", [128, CT * 4]),
                    ("dtb", [128, NVH]), ("alog", [128, NVH]),
                    ("normw", [128, 1]), ("ident", [128, 128]),
                    ("trii", [128, 128]), ("maskl", [128, NVH * 128]),
                    ("masku", [128, NVH * 128]), ("identv", [128, NVH * 128])]:
        ext[nm] = nc.dram_tensor(nm, shp, F32, kind="ExternalInput").ap()
    ext["gout"] = nc.dram_tensor("gatedT", [NZ, NTOK], BF16,
                                 kind="ExternalOutput").ap()
    ext["xlnT"] = nc.dram_tensor("xlnT", [D, NTOK], BF16).ap()
    dbg_t = None
    if dbg:
        dbg_t = {}
        for nm, shp in [("d_ba", [128, 4 * NBA]), ("d_convT", [128, CT * 512]),
                        ("d_qkn", [128, 2 * NKH * 512]),
                        ("d_szT", [128, NVH * 512]),
                        ("d_cg", [128, NVH]), ("d_gam", [128, NVH]),
                        ("d_ghat", [128, NVH]), ("d_gamC", [128, NVH]),
                        ("d_beta", [128, NVH]), ("d_g", [128, NVH]),
                        ("d_rows", [1, 2 * NVH * 128]),
                        ("d_E", [128, NVH * 128]), ("d_Dl", [128, NVH * 128]),
                        ("d_Dui", [128, NVH * 128]), ("d_Dus", [128, NVH * 128]),
                        ("d_M", [128, 128]), ("d_MT", [128, 128]),
                        ("d_B", [128, 128]), ("d_U", [128, 128]),
                        ("d_attnT", [128, 128]), ("d_qbar", [128, 128]),
                        ("d_khat", [128, 128]), ("d_oT", [128, 128]),
                        ("d_S", [128, NZ]), ("d_xlnT", [128, 16 * 512]),
                        ("d_P0", [128, 128]), ("d_Sin", [128, NZ])]:
            dbg_t[nm] = nc.dram_tensor(nm, shp, F32, kind="ExternalOutput").ap()
    p = Prog(nc)
    emit_deltanet(nc, p, ext, NKH, KVR, gate_order, fused=False, dbg_t=dbg_t)
    p.emit()
    return nc


def _tile(v, n=128):
    return np.tile(np.asarray(v, np.float32)[None, :], (n, 1))


def _consts(NVH):
    ident = np.eye(128, dtype=np.float32)
    trii = np.triu(np.ones((128, 128), np.float32))
    ml = np.where(np.tril(np.ones((128, 128), np.float32), -1) > 0, 0.0, NEG)
    mu = np.where(np.triu(np.ones((128, 128), np.float32), 0) > 0, 0.0, NEG)
    return dict(ident=ident, trii=trii,
                maskl=np.tile(ml, (1, NVH)).astype(np.float32),
                masku=np.tile(mu, (1, NVH)).astype(np.float32),
                identv=np.tile(ident, (1, NVH)).astype(np.float32))


def prep_deltanet_a(ins, c, x_flat):
    W = np.asarray(ins["L0_attn_a_W_qkvz"], np.float32)
    Wba = np.asarray(ins["L0_attn_a_W_ba"], np.float32)
    cw = np.asarray(ins["L0_attn_a_conv_w"], np.float32)
    kh = [2 * c, 2 * c + 1]
    vj = [4 * c + r for r in range(4)]
    qc = [W[:, 768 * h:768 * h + 128] for h in kh]
    kc = [W[:, 768 * h + 128:768 * h + 256] for h in kh]
    vc = [W[:, 768 * (j // 2) + 256 + 128 * (j % 2):
            768 * (j // 2) + 256 + 128 * (j % 2) + 128] for j in vj]
    zc = [W[:, 768 * (j // 2) + 512 + 128 * (j % 2):
            768 * (j // 2) + 512 + 128 * (j % 2) + 128] for j in vj]
    wmix = np.concatenate(qc + kc + vc, axis=1)
    wz = np.concatenate(zc, axis=1)
    bcols = [4 * (j // 2) + (j % 2) for j in vj]
    acols = [4 * (j // 2) + 2 + (j % 2) for j in vj]
    wba = Wba[:, bcols + acols]
    rows = (
        [np.arange(128 * h, 128 * h + 128) for h in kh]
        + [np.arange(2048 + 128 * h, 2048 + 128 * h + 128) for h in kh]
        + [np.arange(4096 + 128 * j, 4096 + 128 * j + 128) for j in vj])
    rows = np.concatenate(rows)
    convw = cw[rows].reshape(8, 128, 4).transpose(1, 0, 2).reshape(128, 32)
    d = dict(x=x_flat, wmix=np.ascontiguousarray(wmix),
             wz=np.ascontiguousarray(wz), wba=np.ascontiguousarray(wba),
             convw=np.ascontiguousarray(convw),
             dtb=_tile(ins["L0_attn_a_dt_bias"][vj]),
             alog=_tile(ins["L0_attn_a_A_log"][vj]),
             normw=np.ascontiguousarray(
                 np.asarray(ins["L0_attn_a_norm_w"], np.float32)[:, None]))
    d.update(_consts(4))
    return d


def prep_deltanet_b(ins, c, x_flat):
    W = np.asarray(ins["L0_attn_b_W_qkv"], np.float32)
    cw = np.asarray(ins["L0_attn_b_conv_w"], np.float32)[:, 0, :]
    hh = [2 * c, 2 * c + 1]
    qcset = [W[:, 128 * h:128 * h + 128] for h in hh]
    kcset = [W[:, 2048 + 128 * h:2048 + 128 * h + 128] for h in hh]
    vcset = [W[:, 4096 + 128 * h:4096 + 128 * h + 128] for h in hh]
    wmix = np.concatenate(qcset + kcset + vcset, axis=1)
    Wz = np.asarray(ins["L0_attn_b_W_z"], np.float32)
    wz = np.concatenate([Wz[:, 128 * h:128 * h + 128] for h in hh], axis=1)
    wba = np.concatenate(
        [np.asarray(ins["L0_attn_b_W_b"], np.float32)[:, hh],
         np.asarray(ins["L0_attn_b_W_a"], np.float32)[:, hh]], axis=1)
    rows = (
        [np.arange(128 * h, 128 * h + 128) for h in hh]
        + [np.arange(2048 + 128 * h, 2048 + 128 * h + 128) for h in hh]
        + [np.arange(4096 + 128 * h, 4096 + 128 * h + 128) for h in hh])
    rows = np.concatenate(rows)
    convw = cw[rows].reshape(6, 128, 4).transpose(1, 0, 2).reshape(128, 24)
    d = dict(x=x_flat, wmix=np.ascontiguousarray(wmix),
             wz=np.ascontiguousarray(wz), wba=np.ascontiguousarray(wba),
             convw=np.ascontiguousarray(convw),
             dtb=_tile(np.asarray(ins["L0_attn_b_dt_bias"], np.float32)[hh]),
             alog=_tile(np.asarray(ins["L0_attn_b_A_log"], np.float32)[hh]),
             normw=np.ascontiguousarray(
                 np.asarray(ins["L0_attn_b_norm_w"], np.float32)[:, None]))
    d.update(_consts(2))
    return d


def emit_ln(p, xv, out_ap, stat, tmp1, tmp2, gamma_bc=None, beta_bc=None,
            eps=1e-5):
    DD = 2048
    p.op("vector", lambda e: e.reduce_sum(stat[:, 0:1], xv, axis=AX.X))
    p.bar()
    p.op("vector", lambda e: e.tensor_scalar(stat[:, 1:2], stat[:, 0:1],
                                             -1.0 / DD, None, ALU.mult))
    p.bar()
    p.op("vector", lambda e: e.tensor_scalar(tmp1[:], xv, stat[:, 1:2],
                                             None, ALU.add))
    p.bar()
    p.op("vector", lambda e: e.tensor_tensor(tmp2[:], tmp1[:], tmp1[:], ALU.mult))
    p.bar()
    p.op("vector", lambda e: e.reduce_sum(stat[:, 6:7], tmp2[:], axis=AX.X))
    p.bar()
    p.op("vector", lambda e, eps=eps: e.tensor_scalar(
        stat[:, 6:7], stat[:, 6:7], 1.0 / DD, eps, ALU.mult, ALU.add))
    p.bar()
    p.op("scalar", lambda e: e.activation(stat[:, 7:8], stat[:, 6:7], AF.Sqrt))
    p.bar()
    p.op("vector", lambda e: e.reciprocal(stat[:, 6:7], stat[:, 7:8]))
    p.bar()
    if gamma_bc is None:
        p.op("vector", lambda e: e.tensor_scalar(out_ap, tmp1[:], stat[:, 6:7],
                                                 None, ALU.mult))
    else:
        p.op("vector", lambda e: e.tensor_scalar(tmp2[:], tmp1[:], stat[:, 6:7],
                                                 None, ALU.mult))
        p.bar()
        p.op("vector", lambda e: e.tensor_tensor(tmp1[:], tmp2[:], gamma_bc,
                                                 ALU.mult))
        p.bar()
        p.op("vector", lambda e: e.tensor_tensor(out_ap, tmp1[:], beta_bc,
                                                 ALU.add))
    p.bar()


def emit_transpose16(p, src_bf, dest_view16, pT, idb_sb):
    for half in range(2):
        for k in range(8):
            blk = half * 8 + k
            p.op("tensor", lambda e, blk=blk, k=k:
                 e.transpose(pT[:, k * 128:(k + 1) * 128],
                             src_bf[:, blk * 128:(blk + 1) * 128], idb_sb[:]))
        p.bar()
        dv = dest_view16[:, half * 8:(half + 1) * 8, :]
        p.op("vector", lambda e, dv=dv:
             e.tensor_copy(dv, pT[:, 0:1024].rearrange("q (k c) -> q k c", c=128)))
        p.bar()


def emit_matout(nc, p, ext, KV):
    NKT = KV // 128
    xs = ext["xs"]; gts = ext["gts"]; wout = ext["wout"]; ys = ext["ys"]
    with ExitStack() as st:
        en = st.enter_context
        wo_sb = en(nc.sbuf_tensor("wo_sb", [128, NKT * D], BF16))
        gt_sb = en(nc.sbuf_tensor("gt_sb", [128, 2 * NKT * 128], BF16))
        xt = en(nc.sbuf_tensor("xt", [128, 2 * D], F32))
        yt = en(nc.sbuf_tensor("yt", [128, D], F32))
        ps0 = en(nc.psum_tensor("ps0", [128, 512], F32))
        ps1 = en(nc.psum_tensor("ps1", [128, 512], F32))
        p.dma("gpsimd", wo_sb[:].rearrange("p (kt c) -> p kt c", c=D),
              wout.rearrange("(kt p) c -> p kt c", p=128))
        p.dma("gpsimd", gt_sb[:, 0:NKT * 128].rearrange("p (kt c) -> p kt c", c=128),
              gts[:, 0:128].rearrange("(kt p) c -> p kt c", p=128))
        p.dma("sync", xt[:, 0:D], xs[0:128, :])
        p.bar()
        for tt in range(8):
            par = tt % 2
            if tt + 1 < 8:
                p.dma("gpsimd",
                      gt_sb[:, (1 - par) * NKT * 128:(2 - par) * NKT * 128]
                      .rearrange("p (kt c) -> p kt c", c=128),
                      gts[:, (tt + 1) * 128:(tt + 2) * 128]
                      .rearrange("(kt p) c -> p kt c", p=128))
                p.dma("sync", xt[:, (1 - par) * D:(2 - par) * D],
                      xs[(tt + 1) * 128:(tt + 2) * 128, :])
            for ob in range(4):
                bank = (ps0, ps1)[ob % 2]
                for kt in range(NKT):
                    lh = gt_sb[:, par * NKT * 128 + kt * 128:
                               par * NKT * 128 + (kt + 1) * 128]
                    p.op("tensor", lambda e, lh=lh, kt=kt, ob=ob, bank=bank:
                         e.matmul(bank[:], lh,
                                  wo_sb[:, kt * D + ob * 512:kt * D + (ob + 1) * 512],
                                  start=(kt == 0), stop=(kt == NKT - 1)))
                p.bar()
                xv = xt[:, par * D + ob * 512:par * D + (ob + 1) * 512]
                p.op("vector", lambda e, ob=ob, xv=xv, bank=bank:
                     e.tensor_tensor(yt[:, ob * 512:(ob + 1) * 512], bank[:],
                                     xv, ALU.add))
                p.bar()
            p.dma("sync", ys[tt * 128:(tt + 1) * 128, :], yt[:])
            p.bar()


def build_matout(KV):
    nc = bass.Bass("TRN2", target_bir_lowering=False)
    ext = dict(
        xs=nc.dram_tensor("xs", [RPC, D], F32, kind="ExternalInput").ap(),
        gts=nc.dram_tensor("gts", [KV, RPC], BF16, kind="ExternalInput").ap(),
        wout=nc.dram_tensor("wout", [KV, D], F32, kind="ExternalInput").ap(),
        ys=nc.dram_tensor("ys", [RPC, D], F32, kind="ExternalOutput").ap())
    p = Prog(nc)
    emit_matout(nc, p, ext, KV)
    p.emit()
    return nc


def emit_ffn0(nc, p, ext):
    NIT = 44
    xs = ext["xs"]; gwT = ext["gwT"]; uwT = ext["uwT"]; dwT = ext["dwT"]
    gsc = ext["gsc"]; usc = ext["usc"]; dsb = ext["dsb"]; ident = ext["ident"]
    ys = ext["ys"]
    with ExitStack() as st:
        en = st.enter_context
        xlnT = en(nc.sbuf_tensor("xlnT", [128, 16 * RPC], BF16))
        hT = en(nc.sbuf_tensor("hT", [128, NIT * RPC], BF16))
        wt = en(nc.sbuf_tensor("wt", [128, 2 * 16 * 128], BF16))
        dwt = en(nc.sbuf_tensor("dwt", [128, NIT * 512], BF16))
        gsc_sb = en(nc.sbuf_tensor("gsc_sb", [128, NIT], F32))
        usc_sb = en(nc.sbuf_tensor("usc_sb", [128, NIT], F32))
        dsb_sb = en(nc.sbuf_tensor("dsb_sb", [128, 512], F32))
        idb_sb = en(nc.sbuf_tensor("idb_sb", [128, 128], BF16))
        xt = en(nc.sbuf_tensor("xt", [128, D], F32))
        stat = en(nc.sbuf_tensor("stat", [128, 8], F32))
        tmp1 = en(nc.sbuf_tensor("tmp1", [128, D], F32))
        tmp2 = en(nc.sbuf_tensor("tmp2", [128, D], F32))
        xlnb = en(nc.sbuf_tensor("xlnb", [128, D], BF16))
        scrA = en(nc.sbuf_tensor("scrA", [128, 512], F32))
        yt = en(nc.sbuf_tensor("yt", [128, 512], F32))
        ps0 = en(nc.psum_tensor("ps0", [128, 512], F32))
        ps1 = en(nc.psum_tensor("ps1", [128, 512], F32))
        pT = en(nc.psum_tensor("pT", [128, 1024], BF16))
        scrB = tmp2
        p.dma("sync", gsc_sb[:], gsc)
        p.dma("sync", usc_sb[:], usc)
        p.dma("gpsimd", idb_sb[:], ident)
        p.bar()
        for i in range(8):
            xv = xt[:, 0:D]
            p.dma("sync", xt[:, 0:D], xs[i * 128:(i + 1) * 128, :])
            p.bar()
            emit_ln(p, xv, xlnb[:], stat, tmp1, tmp2)
            emit_transpose16(
                p, xlnb[:],
                xlnT[:].rearrange("q (dt c) -> q dt c", c=RPC)[:, :, i * 128:(i + 1) * 128],
                pT, idb_sb)
        for it in range(NIT):
            base = 0
            p.dma("gpsimd", wt[:, 0:2048].rearrange("p (dt c) -> p dt c", c=128),
                  gwT[:, it * 128:(it + 1) * 128]
                  .rearrange("(dt p) c -> p dt c", p=128))
            p.dma("gpsimd", wt[:, 2048:4096].rearrange("p (dt c) -> p dt c", c=128),
                  uwT[:, it * 128:(it + 1) * 128]
                  .rearrange("(dt p) c -> p dt c", p=128))
            p.bar()
            for th in range(2):
                for dt in range(16):
                    rv = xlnT[:, dt * RPC + th * 512:dt * RPC + (th + 1) * 512]
                    p.op("tensor", lambda e, dt=dt, base=base, rv=rv:
                         e.matmul(ps0[:], wt[:, base + dt * 128:base + (dt + 1) * 128],
                                  rv, start=(dt == 0), stop=(dt == 15)))
                    p.op("tensor", lambda e, dt=dt, base=base, rv=rv:
                         e.matmul(ps1[:], wt[:, base + 2048 + dt * 128:
                                             base + 2048 + (dt + 1) * 128],
                                  rv, start=(dt == 0), stop=(dt == 15)))
                p.bar()
                p.op("vector", lambda e, it=it:
                     e.tensor_scalar(scrA[:], ps0[:], gsc_sb[:, it:it + 1],
                                     None, ALU.mult))
                p.op("vector", lambda e, it=it:
                     e.tensor_scalar(scrB[:, 0:512], ps1[:], usc_sb[:, it:it + 1],
                                     None, ALU.mult))
                p.bar()
                p.op("scalar", lambda e: e.activation(scrA[:], scrA[:], AF.Silu))
                p.bar()
                hv = hT[:, it * RPC + th * 512:it * RPC + (th + 1) * 512]
                p.op("vector", lambda e, hv=hv:
                     e.tensor_tensor(hv, scrA[:], scrB[:, 0:512], ALU.mult))
                p.bar()
        p.dma("sync", xt[:, 0:D], xs[0:128, :])
        p.bar()
        for ob in range(4):
            p.dma("gpsimd", dwt[:].rearrange("p (kt c) -> p kt c", c=512),
                  dwT[:, ob * 512:(ob + 1) * 512]
                  .rearrange("(kt p) c -> p kt c", p=128))
            p.dma("sync", dsb_sb[:], dsb[:, ob * 512:(ob + 1) * 512])
            p.bar()
            for tt in range(8):
                bank = (ps0, ps1)[tt % 2]
                for kt in range(NIT):
                    lh = hT[:, kt * RPC + tt * 128:kt * RPC + (tt + 1) * 128]
                    p.op("tensor", lambda e, lh=lh, kt=kt, bank=bank:
                         e.matmul(bank[:], lh,
                                  dwt[:, kt * 512:(kt + 1) * 512],
                                  start=(kt == 0), stop=(kt == NIT - 1)))
                p.dma("sync", xt[:, ob * 512:(ob + 1) * 512],
                      xs[tt * 128:(tt + 1) * 128, ob * 512:(ob + 1) * 512])
                p.bar()
                p.op("vector", lambda e, bank=bank:
                     e.tensor_tensor(yt[:], bank[:], dsb_sb[:], ALU.mult))
                p.bar()
                xv = xt[:, ob * 512:(ob + 1) * 512]
                p.op("vector", lambda e, xv=xv: e.tensor_tensor(yt[:], yt[:],
                                                                xv, ALU.add))
                p.bar()
                p.dma("sync", ys[tt * 128:(tt + 1) * 128, ob * 512:(ob + 1) * 512],
                      yt[:])
            p.bar()


def build_ffn0():
    nc = bass.Bass("TRN2", target_bir_lowering=False)
    ext = dict(
        xs=nc.dram_tensor("xs", [RPC, D], F32, kind="ExternalInput").ap(),
        gwT=nc.dram_tensor("gwT", [D, 5632], F32, kind="ExternalInput").ap(),
        uwT=nc.dram_tensor("uwT", [D, 5632], F32, kind="ExternalInput").ap(),
        dwT=nc.dram_tensor("dwT", [5632, D], F32, kind="ExternalInput").ap(),
        gsc=nc.dram_tensor("gsc", [128, 44], F32, kind="ExternalInput").ap(),
        usc=nc.dram_tensor("usc", [128, 44], F32, kind="ExternalInput").ap(),
        dsb=nc.dram_tensor("dsb", [128, D], F32, kind="ExternalInput").ap(),
        ident=nc.dram_tensor("ident", [128, 128], F32, kind="ExternalInput").ap(),
        ys=nc.dram_tensor("ys", [RPC, D], F32, kind="ExternalOutput").ap())
    p = Prog(nc)
    emit_ffn0(nc, p, ext)
    p.emit()
    return nc


def emit_attn1a(nc, p, ext):
    xs = ext["xs"]; wqkvT = ext["wqkvT"]; wprojT = ext["wprojT"]
    ident = ext["ident"]; ys = ext["ys"]
    PAIRS = [(i, j) for i in range(4) for j in range(i + 1)]
    SC = 1.0 / np.sqrt(128.0)
    with ExitStack() as st:
        en = st.enter_context
        xlnT = en(nc.sbuf_tensor("xlnT", [128, 16 * RPC], BF16))
        qkvT = en(nc.sbuf_tensor("qkvT", [128, 24 * RPC], BF16))
        attnT = en(nc.sbuf_tensor("attnT", [128, 16 * RPC], BF16))
        wq_sb = en(nc.sbuf_tensor("wq_sb", [128, 16 * 128], BF16))
        wp_sb = en(nc.sbuf_tensor("wp_sb", [128, 16 * 512], BF16))
        idb_sb = en(nc.sbuf_tensor("idb_sb", [128, 128], BF16))
        ones_cf = en(nc.sbuf_tensor("ones_cf", [128, 1], F32))
        ones_rb = en(nc.sbuf_tensor("ones_rb", [1, 128], BF16))
        xt = en(nc.sbuf_tensor("xt", [128, D], F32))
        stat = en(nc.sbuf_tensor("stat", [128, 8], F32))
        tmp1 = en(nc.sbuf_tensor("tmp1", [128, D], F32))
        tmp2 = en(nc.sbuf_tensor("tmp2", [128, D], F32))
        xlnb = en(nc.sbuf_tensor("xlnb", [128, D], BF16))
        scrP = en(nc.sbuf_tensor("scrP", [128, 256], F32))
        srow = en(nc.sbuf_tensor("srow", [1, 10 * 256], F32))
        erow = en(nc.sbuf_tensor("erow", [1, 10 * 256], F32))
        prow = en(nc.sbuf_tensor("prow", [1, 10 * 256], BF16))
        mrow = en(nc.sbuf_tensor("mrow", [1, 4 * 256], F32))
        accA = en(nc.sbuf_tensor("accA", [128, 2 * 256], F32))
        yt = en(nc.sbuf_tensor("yt", [128, 512], F32))
        ps0 = en(nc.psum_tensor("ps0", [128, 512], F32))
        ps1 = en(nc.psum_tensor("ps1", [128, 512], F32))
        pR = en(nc.psum_tensor("pR", [128, 512], F32))
        pB = en(nc.psum_tensor("pB", [128, 512], F32))
        pT = en(nc.psum_tensor("pT", [128, 1024], BF16))
        p.dma("gpsimd", idb_sb[:], ident)
        p.op("vector", lambda e: e.memset(ones_cf[:], 1.0))
        p.op("vector", lambda e: e.memset(ones_rb[:], 1.0))
        p.bar()
        for i in range(8):
            xv = xt[:, 0:D]
            p.dma("sync", xt[:, 0:D], xs[i * 128:(i + 1) * 128, :])
            p.bar()
            emit_ln(p, xv, xlnb[:], stat, tmp1, tmp2)
            emit_transpose16(
                p, xlnb[:],
                xlnT[:].rearrange("q (dt c) -> q dt c", c=RPC)
                [:, :, i * 128:(i + 1) * 128], pT, idb_sb)
        for ctile in range(24):
            base = 0
            p.dma("gpsimd", wq_sb[:, 0:2048].rearrange("p (dt c) -> p dt c", c=128),
                  wqkvT[:, ctile * 128:(ctile + 1) * 128]
                  .rearrange("(dt p) c -> p dt c", p=128))
            p.bar()
            for th in range(2):
                bank = (ps0, ps1)[th]
                for dt in range(16):
                    rv = xlnT[:, dt * RPC + th * 512:dt * RPC + (th + 1) * 512]
                    p.op("tensor", lambda e, dt=dt, base=base, rv=rv, bank=bank:
                         e.matmul(bank[:],
                                  wq_sb[:, base + dt * 128:base + (dt + 1) * 128],
                                  rv, start=(dt == 0), stop=(dt == 15)))
            p.bar()
            for th in range(2):
                bank = (ps0, ps1)[th]
                qv = qkvT[:, ctile * RPC + th * 512:ctile * RPC + (th + 1) * 512]
                p.op("vector", lambda e, qv=qv, bank=bank:
                     e.tensor_copy(qv, bank[:]))
            p.bar()
        for h in range(16):
            g = h // 4
            qt = 6 * g + (h % 4)
            kt = 6 * g + 4
            vt = 6 * g + 5
            for pi, (i, j) in enumerate(PAIRS):
                qv = qkvT[:, qt * RPC + 256 * i:qt * RPC + 256 * i + 256]
                kv = qkvT[:, kt * RPC + 256 * j:kt * RPC + 256 * j + 256]
                p.op("vector", lambda e, qv=qv, kv=kv:
                     e.tensor_tensor(scrP[:], qv, kv, ALU.mult))
                p.bar()
                p.op("tensor", lambda e:
                     e.matmul(pR[0:1, 0:256], ones_cf[:], scrP[:],
                              start=True, stop=True))
                p.bar()
                sv = srow[0:1, pi * 256:(pi + 1) * 256]
                p.op("vector", lambda e, sv=sv: e.tensor_copy(sv, pR[0:1, 0:256]))
                p.bar()
            for i in range(4):
                pis = [pi for pi, (a, b2) in enumerate(PAIRS) if a == i]
                mv = mrow[0:1, i * 256:(i + 1) * 256]
                p.op("vector", lambda e, mv=mv, pi0=pis[0]:
                     e.tensor_copy(mv, srow[0:1, pi0 * 256:(pi0 + 1) * 256]))
                p.bar()
                for pi in pis[1:]:
                    p.op("vector", lambda e, mv=mv, pi=pi:
                         e.tensor_tensor(mv, mv, srow[0:1, pi * 256:(pi + 1) * 256],
                                         ALU.max))
                    p.bar()
                for pi in pis:
                    ev = erow[0:1, pi * 256:(pi + 1) * 256]
                    p.op("vector", lambda e, ev=ev, pi=pi, mv=mv:
                         e.tensor_tensor(ev, srow[0:1, pi * 256:(pi + 1) * 256],
                                         mv, ALU.subtract))
                p.bar()
                for pi in pis:
                    ev = erow[0:1, pi * 256:(pi + 1) * 256]
                    p.op("scalar", lambda e, ev=ev:
                         e.activation(ev, ev, AF.Exp, scale=SC))
                p.bar()
                p.op("vector", lambda e, mv=mv, pi0=pis[0]:
                     e.tensor_copy(mv, erow[0:1, pis[0] * 256:(pis[0] + 1) * 256])
                     if False else
                     e.tensor_copy(mv, erow[0:1, pi0 * 256:(pi0 + 1) * 256]))
                p.bar()
                for pi in pis[1:]:
                    p.op("vector", lambda e, mv=mv, pi=pi:
                         e.tensor_tensor(mv, mv, erow[0:1, pi * 256:(pi + 1) * 256],
                                         ALU.add))
                    p.bar()
                p.op("vector", lambda e, mv=mv: e.reciprocal(mv, mv))
                p.bar()
                for pi in pis:
                    pv = prow[0:1, pi * 256:(pi + 1) * 256]
                    p.op("vector", lambda e, pv=pv, pi=pi, mv=mv:
                         e.tensor_tensor(pv, erow[0:1, pi * 256:(pi + 1) * 256],
                                         mv, ALU.mult))
                p.bar()
            for i in range(4):
                pis = [(pi, jj) for pi, (a, jj) in enumerate(PAIRS) if a == i]
                ap = 0
                for n, (pi, j) in enumerate(pis):
                    p.op("tensor", lambda e, pi=pi:
                         e.matmul(pB[:, 0:256], ones_rb[:],
                                  prow[0:1, pi * 256:(pi + 1) * 256],
                                  start=True, stop=True))
                    p.bar()
                    vv = qkvT[:, vt * RPC + 256 * j:vt * RPC + 256 * j + 256]
                    if n == 0:
                        p.op("vector", lambda e, vv=vv:
                             e.tensor_tensor(accA[:, 0:256], vv, pB[:, 0:256],
                                             ALU.mult))
                    else:
                        src = accA[:, (ap % 2) * 256:(ap % 2) * 256 + 256]
                        dst = accA[:, ((ap + 1) % 2) * 256:((ap + 1) % 2) * 256 + 256]
                        p.op("vector", lambda e, vv=vv:
                             e.tensor_tensor(tmp1[:, 0:256], vv, pB[:, 0:256],
                                             ALU.mult))
                        p.bar()
                        p.op("vector", lambda e, src=src, dst=dst:
                             e.tensor_tensor(dst, src, tmp1[:, 0:256], ALU.add))
                        ap += 1
                    p.bar()
                av = attnT[:, h * RPC + 256 * i:h * RPC + 256 * i + 256]
                fin = accA[:, (ap % 2) * 256:(ap % 2) * 256 + 256]
                p.op("vector", lambda e, av=av, fin=fin: e.tensor_copy(av, fin))
                p.bar()
        for ob in range(4):
            p.dma("gpsimd", wp_sb[:].rearrange("p (dt c) -> p dt c", c=512),
                  wprojT[:, ob * 512:(ob + 1) * 512]
                  .rearrange("(dt p) c -> p dt c", p=128))
            p.bar()
            for tt in range(8):
                p.dma("sync", xt[:, 0:D], xs[tt * 128:(tt + 1) * 128, :])
                p.bar()
                bank = (ps0, ps1)[tt % 2]
                for dt in range(16):
                    lh = attnT[:, dt * RPC + tt * 128:dt * RPC + (tt + 1) * 128]
                    p.op("tensor", lambda e, lh=lh, dt=dt, bank=bank:
                         e.matmul(bank[:], lh,
                                  wp_sb[:, dt * 512:(dt + 1) * 512],
                                  start=(dt == 0), stop=(dt == 15)))
                p.bar()
                xv = xt[:, ob * 512:(ob + 1) * 512]
                p.op("vector", lambda e, xv=xv, bank=bank:
                     e.tensor_tensor(yt[:], bank[:], xv, ALU.add))
                p.bar()
                p.dma("sync", ys[tt * 128:(tt + 1) * 128, ob * 512:(ob + 1) * 512],
                      yt[:])
        p.bar()


def build_attn1a():
    nc = bass.Bass("TRN2", target_bir_lowering=False)
    ext = dict(
        xs=nc.dram_tensor("xs", [RPC, D], F32, kind="ExternalInput").ap(),
        wqkvT=nc.dram_tensor("wqkvT", [D, 3072], F32, kind="ExternalInput").ap(),
        wprojT=nc.dram_tensor("wprojT", [D, D], F32, kind="ExternalInput").ap(),
        ident=nc.dram_tensor("ident", [128, 128], F32, kind="ExternalInput").ap(),
        ys=nc.dram_tensor("ys", [RPC, D], F32, kind="ExternalOutput").ap())
    p = Prog(nc)
    emit_attn1a(nc, p, ext)
    p.emit()
    return nc


def emit_moe(nc, p, ext):
    xs = ext["xs"]; wg = ext["wg"]; wu = ext["wu"]; wd = ext["wd"]
    sg = ext["sg"]; su = ext["su"]; sd = ext["sd"]; wr = ext["wr"]
    wsg = ext["wsg"]; ident = ext["ident"]; ys = ext["ys"]
    with ExitStack() as st:
        en = st.enter_context
        xlnT = en(nc.sbuf_tensor("xlnT", [128, 16 * RPC], BF16))
        xacc = en(nc.sbuf_tensor("xacc", [128, 8 * D], F32))
        hT = en(nc.sbuf_tensor("hT", [128, 22 * RPC], BF16))
        wt = en(nc.sbuf_tensor("wt", [128, 2 * 16 * 128], BF16))
        dwt = en(nc.sbuf_tensor("dwt", [128, 22 * 512], BF16))
        wr_sb = en(nc.sbuf_tensor("wr_sb", [128, 16 * 8], BF16))
        wsg_sb = en(nc.sbuf_tensor("wsg_sb", [128, 16], BF16))
        C_sb = en(nc.sbuf_tensor("C_sb", [128, 8 * 8], F32))
        sgate = en(nc.sbuf_tensor("sgate", [128, 8], F32))
        lg = en(nc.sbuf_tensor("lg", [128, 6 * 8], F32))
        mcol = en(nc.sbuf_tensor("mcol", [128, 4], F32))
        idb_sb = en(nc.sbuf_tensor("idb_sb", [128, 128], BF16))
        stat = en(nc.sbuf_tensor("stat", [128, 8], F32))
        tmp1 = en(nc.sbuf_tensor("tmp1", [128, D], F32))
        tmp2 = en(nc.sbuf_tensor("tmp2", [128, D], F32))
        xlnb = en(nc.sbuf_tensor("xlnb", [128, D], BF16))
        scrA = en(nc.sbuf_tensor("scrA", [128, 512], F32))
        ps0 = en(nc.psum_tensor("ps0", [128, 512], F32))
        ps1 = en(nc.psum_tensor("ps1", [128, 512], F32))
        pT = en(nc.psum_tensor("pT", [128, 1024], BF16))
        p.dma("gpsimd", idb_sb[:], ident)
        p.dma("gpsimd", wr_sb[:].rearrange("p (dt c) -> p dt c", c=8),
              wr.rearrange("(dt p) c -> p dt c", p=128))
        p.dma("gpsimd", wsg_sb[:], wsg)
        p.dma("sync", xacc[:].rearrange("p (tt c) -> p tt c", c=D),
              xs.rearrange("(tt p) c -> p tt c", p=128))
        p.bar()
        for i in range(8):
            xv = xacc[:, i * D:(i + 1) * D]
            emit_ln(p, xv, xlnb[:], stat, tmp1, tmp2)
            emit_transpose16(
                p, xlnb[:],
                xlnT[:].rearrange("q (dt c) -> q dt c", c=RPC)
                [:, :, i * 128:(i + 1) * 128], pT, idb_sb)
        for tt in range(8):
            for dt in range(16):
                lh = xlnT[:, dt * RPC + tt * 128:dt * RPC + (tt + 1) * 128]
                p.op("tensor", lambda e, lh=lh, dt=dt:
                     e.matmul(ps0[:, 0:8], lh, wr_sb[:, dt * 8:(dt + 1) * 8],
                              start=(dt == 0), stop=(dt == 15)))
                p.op("tensor", lambda e, lh=lh, dt=dt:
                     e.matmul(ps1[:, 0:1], lh, wsg_sb[:, dt:dt + 1],
                              start=(dt == 0), stop=(dt == 15)))
            p.bar()
            p.op("vector", lambda e: e.tensor_copy(lg[:, 0:8], ps0[:, 0:8]))
            p.op("scalar", lambda e, tt=tt:
                 e.activation(sgate[:, tt:tt + 1], ps1[:, 0:1], AF.Sigmoid))
            p.bar()
            p.op("vector", lambda e: e.reduce_max(mcol[:, 0:1], lg[:, 0:8],
                                                  axis=AX.X))
            p.bar()
            p.op("vector", lambda e: e.tensor_scalar(lg[:, 8:16], lg[:, 0:8],
                                                     mcol[:, 0:1], None,
                                                     ALU.subtract))
            p.bar()
            p.op("scalar", lambda e: e.activation(lg[:, 8:16], lg[:, 8:16], AF.Exp))
            p.bar()
            p.op("vector", lambda e: e.reduce_sum(mcol[:, 1:2], lg[:, 8:16],
                                                  axis=AX.X))
            p.bar()
            p.op("vector", lambda e: e.reciprocal(mcol[:, 2:3], mcol[:, 1:2]))
            p.bar()
            p.op("vector", lambda e: e.tensor_scalar(lg[:, 16:24], lg[:, 8:16],
                                                     mcol[:, 2:3], None, ALU.mult))
            p.bar()
            p.op("vector", lambda e: e.reduce_max(mcol[:, 0:1], lg[:, 16:24],
                                                  axis=AX.X))
            p.bar()
            p.op("vector", lambda e: e.tensor_scalar(lg[:, 24:32], lg[:, 16:24],
                                                     mcol[:, 0:1], None, ALU.is_ge))
            p.bar()
            p.op("vector", lambda e: e.scalar_tensor_tensor(
                lg[:, 32:40], lg[:, 24:32], -1e30, lg[:, 16:24], ALU.mult, ALU.add))
            p.bar()
            p.op("vector", lambda e: e.reduce_max(mcol[:, 3:4], lg[:, 32:40],
                                                  axis=AX.X))
            p.bar()
            p.op("vector", lambda e: e.tensor_scalar(lg[:, 40:48], lg[:, 16:24],
                                                     mcol[:, 3:4], None, ALU.is_ge))
            p.bar()
            p.op("vector", lambda e, tt=tt: e.tensor_tensor(
                C_sb[:, tt * 8:(tt + 1) * 8], lg[:, 16:24], lg[:, 40:48], ALU.mult))
            p.bar()
        def ffn_pass(src_g, src_u, src_d, nit, half, scale_mode):
            i0 = half * nit
            for it in range(nit):
                base = 0
                p.dma("gpsimd", wt[:, 0:2048].rearrange("p (dt c) -> p dt c", c=128),
                      src_g[:, (i0 + it) * 128:(i0 + it + 1) * 128]
                      .rearrange("(dt p) c -> p dt c", p=128))
                p.dma("gpsimd", wt[:, 2048:4096]
                      .rearrange("p (dt c) -> p dt c", c=128),
                      src_u[:, (i0 + it) * 128:(i0 + it + 1) * 128]
                      .rearrange("(dt p) c -> p dt c", p=128))
                p.bar()
                for th in range(2):
                    for dt in range(16):
                        rv = xlnT[:, dt * RPC + th * 512:dt * RPC + (th + 1) * 512]
                        p.op("tensor", lambda e, dt=dt, base=base, rv=rv:
                             e.matmul(ps0[:],
                                      wt[:, base + dt * 128:base + (dt + 1) * 128],
                                      rv, start=(dt == 0), stop=(dt == 15)))
                        p.op("tensor", lambda e, dt=dt, base=base, rv=rv:
                             e.matmul(ps1[:], wt[:, base + 2048 + dt * 128:
                                                 base + 2048 + (dt + 1) * 128],
                                      rv, start=(dt == 0), stop=(dt == 15)))
                    p.bar()
                    p.op("scalar", lambda e: e.activation(scrA[:], ps0[:], AF.Silu))
                    p.bar()
                    hv = hT[:, it * RPC + th * 512:it * RPC + (th + 1) * 512]
                    p.op("vector", lambda e, hv=hv:
                         e.tensor_tensor(hv, scrA[:], ps1[:], ALU.mult))
                    p.bar()
            for ob in range(4):
                p.dma("gpsimd", dwt[:, 0:nit * 512]
                      .rearrange("p (kt c) -> p kt c", c=512),
                      src_d[i0 * 128:(i0 + nit) * 128, ob * 512:(ob + 1) * 512]
                      .rearrange("(kt p) c -> p kt c", p=128))
                p.bar()
                for tt in range(8):
                    bank = (ps0, ps1)[tt % 2]
                    for kt in range(nit):
                        lh = hT[:, kt * RPC + tt * 128:kt * RPC + (tt + 1) * 128]
                        p.op("tensor", lambda e, lh=lh, kt=kt, bank=bank, nit=nit:
                             e.matmul(bank[:], lh,
                                      dwt[:, kt * 512:(kt + 1) * 512],
                                      start=(kt == 0), stop=(kt == nit - 1)))
                    p.bar()
                    xv = xacc[:, tt * D + ob * 512:tt * D + (ob + 1) * 512]
                    scal = (C_sb[:, 8 * tt + scale_mode:8 * tt + scale_mode + 1]
                            if scale_mode >= 0 else sgate[:, tt:tt + 1])
                    p.op("vector", lambda e, xv=xv, scal=scal, bank=bank:
                         e.scalar_tensor_tensor(xv, bank[:], scal, xv,
                                                ALU.mult, ALU.add))
                    p.bar()
        for ex in range(8):
            ffn_pass(wg[ex], wu[ex], wd[ex], 11, 0, ex)
        for half in range(2):
            ffn_pass(sg, su, sd, 22, half, -1)
        for tt in range(8):
            p.dma("sync", ys[tt * 128:(tt + 1) * 128, :], xacc[:, tt * D:(tt + 1) * D])
        p.bar()


def build_moe():
    nc = bass.Bass("TRN2", target_bir_lowering=False)
    ext = dict(
        xs=nc.dram_tensor("xs", [RPC, D], F32, kind="ExternalInput").ap(),
        wg=nc.dram_tensor("wg", [8, D, 1408], F32, kind="ExternalInput").ap(),
        wu=nc.dram_tensor("wu", [8, D, 1408], F32, kind="ExternalInput").ap(),
        wd=nc.dram_tensor("wd", [8, 1408, D], F32, kind="ExternalInput").ap(),
        sg=nc.dram_tensor("sg", [D, 5632], F32, kind="ExternalInput").ap(),
        su=nc.dram_tensor("su", [D, 5632], F32, kind="ExternalInput").ap(),
        sd=nc.dram_tensor("sd", [5632, D], F32, kind="ExternalInput").ap(),
        wr=nc.dram_tensor("wr", [D, 8], F32, kind="ExternalInput").ap(),
        wsg=nc.dram_tensor("wsg", [128, 16], F32, kind="ExternalInput").ap(),
        ident=nc.dram_tensor("ident", [128, 128], F32, kind="ExternalInput").ap(),
        ys=nc.dram_tensor("ys", [RPC, D], F32, kind="ExternalOutput").ap())
    p = Prog(nc)
    emit_moe(nc, p, ext)
    p.emit()
    return nc


def emit_attn1b_partial(nc, p, ext):
    xs = ext["xs"]; wkv = ext["wkv"]; bkv = ext["bkv"]; gam = ext["gam"]
    bet = ext["bet"]; coss = ext["coss"]; sins = ext["sins"]
    ident = ext["ident"]; praw = ext["praw"]
    with ExitStack() as st:
        en = st.enter_context
        wkv_sb = en(nc.sbuf_tensor("wkv_sb", [128, 16 * 512], BF16))
        bkv_sb = en(nc.sbuf_tensor("bkv_sb", [128, 4096], F32))
        gam_sb = en(nc.sbuf_tensor("gam_sb", [128, D], F32))
        bet_sb = en(nc.sbuf_tensor("bet_sb", [128, D], F32))
        coss_sb = en(nc.sbuf_tensor("coss_sb", [128, 1024], F32))
        sins_sb = en(nc.sbuf_tensor("sins_sb", [128, 1024], F32))
        idb_sb = en(nc.sbuf_tensor("idb_sb", [128, 128], BF16))
        ones_cf = en(nc.sbuf_tensor("ones_cf", [128, 1], F32))
        xt = en(nc.sbuf_tensor("xt", [128, D], F32))
        stat = en(nc.sbuf_tensor("stat", [128, 8], F32))
        tmp1 = en(nc.sbuf_tensor("tmp1", [128, D], F32))
        tmp2 = en(nc.sbuf_tensor("tmp2", [128, D], F32))
        xln1 = en(nc.sbuf_tensor("xln1", [128, D], F32))
        xnb = en(nc.sbuf_tensor("xnb", [128, D], BF16))
        XnT = en(nc.sbuf_tensor("XnT", [128, 16 * 128], BF16))
        K_sb = en(nc.sbuf_tensor("K_sb", [128, D], F32))
        V_sb = en(nc.sbuf_tensor("V_sb", [128, D], F32))
        Kr = en(nc.sbuf_tensor("Kr", [128, D], F32))
        Vr = en(nc.sbuf_tensor("Vr", [128, D], F32))
        row_sb = en(nc.sbuf_tensor("row_sb", [1, D], F32))
        ps0 = en(nc.psum_tensor("ps0", [128, 512], F32))
        ps1 = en(nc.psum_tensor("ps1", [128, 512], F32))
        pR = en(nc.psum_tensor("pR", [128, 512], F32))
        pT = en(nc.psum_tensor("pT", [128, 1024], BF16))
        p.dma("sync", bkv_sb[:], bkv)
        p.dma("sync", gam_sb[:], gam)
        p.dma("sync", bet_sb[:], bet)
        p.dma("sync", coss_sb[:], coss)
        p.dma("sync", sins_sb[:], sins)
        p.dma("gpsimd", idb_sb[:], ident)
        p.op("vector", lambda e: e.memset(ones_cf[:], 1.0))
        p.bar()
        for i in range(8):
            s = i % 2
            xv = xt[:, 0:D]
            p.dma("sync", xt[:, 0:D], xs[i * 128:(i + 1) * 128, :])
            p.bar()
            emit_ln(p, xv, xln1[:], stat, tmp1, tmp2)
            emit_ln(p, xln1[:], xnb[:], stat, tmp1, tmp2, gam_sb[:], bet_sb[:])
            emit_transpose16(p, xnb[:],
                             XnT[:].rearrange("q (dt c) -> q dt c", c=128),
                             pT, idb_sb)
            for kv_i, dest in ((0, K_sb), (1, V_sb)):
                for ob in range(4):
                    bank = (ps0, ps1)[ob % 2]
                    cc = kv_i * 2048 + ob * 512
                    p.dma("gpsimd",
                          wkv_sb[:].rearrange("p (dt c) -> p dt c", c=512),
                          wkv[:, cc:cc + 512].rearrange("(dt p) c -> p dt c", p=128))
                    p.bar()
                    for dt in range(16):
                        p.op("tensor", lambda e, dt=dt, bank=bank:
                             e.matmul(bank[:], XnT[:, dt * 128:(dt + 1) * 128],
                                      wkv_sb[:, dt * 512:(dt + 1) * 512],
                                      start=(dt == 0), stop=(dt == 15)))
                    p.bar()
                    p.op("vector", lambda e, dest=dest, ob=ob, cc=cc, bank=bank:
                         e.tensor_tensor(dest[:, ob * 512:(ob + 1) * 512], bank[:],
                                         bkv_sb[:, cc:cc + 512], ALU.add))
                    p.bar()
            for src2, dst in ((K_sb, Kr), (V_sb, Vr)):
                se = src2[:, 0:1024].rearrange("p (c two) -> p c two", two=2)
                de = dst[:, 0:1024].rearrange("p (c two) -> p c two", two=2)
                cv = coss_sb[:, s * 512:(s + 1) * 512].unsqueeze(-1)
                sv = sins_sb[:, s * 512:(s + 1) * 512].unsqueeze(-1)
                p.op("vector", lambda e, se=se, cv=cv:
                     e.tensor_tensor(tmp1[:, 0:512].unsqueeze(-1),
                                     se[:, :, 0:1], cv, ALU.mult))
                p.op("vector", lambda e, se=se, sv=sv:
                     e.tensor_tensor(tmp1[:, 512:1024].unsqueeze(-1),
                                     se[:, :, 1:2], sv, ALU.mult))
                p.op("vector", lambda e, se=se, cv=cv:
                     e.tensor_tensor(tmp2[:, 0:512].unsqueeze(-1),
                                     se[:, :, 1:2], cv, ALU.mult))
                p.op("vector", lambda e, se=se, sv=sv:
                     e.tensor_tensor(tmp2[:, 512:1024].unsqueeze(-1),
                                     se[:, :, 0:1], sv, ALU.mult))
                p.bar()
                p.op("vector", lambda e, de=de:
                     e.tensor_tensor(de[:, :, 0:1],
                                     tmp1[:, 0:512].unsqueeze(-1),
                                     tmp1[:, 512:1024].unsqueeze(-1), ALU.subtract))
                p.op("vector", lambda e, de=de:
                     e.tensor_tensor(de[:, :, 1:2],
                                     tmp2[:, 0:512].unsqueeze(-1),
                                     tmp2[:, 512:1024].unsqueeze(-1), ALU.add))
                p.op("vector", lambda e, src2=src2, dst=dst:
                     e.tensor_copy(dst[:, 1024:2048], src2[:, 1024:2048]))
                p.bar()
            for buf in (Kr, Vr):
                p.op("vector", lambda e, buf=buf:
                     e.tensor_tensor(tmp1[:], buf[:], buf[:], ALU.mult))
                p.bar()
                p.op("vector", lambda e: e.reduce_sum(stat[:, 0:1], tmp1[:],
                                                      axis=AX.X))
                p.bar()
                p.op("vector", lambda e: e.reciprocal(stat[:, 1:2], stat[:, 0:1]))
                p.bar()
                p.op("scalar", lambda e: e.activation(stat[:, 2:3], stat[:, 1:2],
                                                      AF.Sqrt))
                p.bar()
                p.op("vector", lambda e, buf=buf:
                     e.tensor_scalar(buf[:], buf[:], stat[:, 2:3], None, ALU.mult))
                p.bar()
            p.op("vector", lambda e: e.tensor_tensor(tmp2[:], Kr[:], Vr[:],
                                                     ALU.mult))
            p.bar()
            for ob in range(4):
                p.op("tensor", lambda e, ob=ob:
                     e.matmul(pR[0:1, 0:512],
                              ones_cf[:], tmp2[:, ob * 512:(ob + 1) * 512],
                              start=True, stop=True))
                p.bar()
                pv = row_sb[0:1, ob * 512:(ob + 1) * 512]
                p.op("vector", lambda e, pv=pv: e.tensor_copy(pv, pR[0:1, 0:512]))
                p.bar()
            p.dma("sync", praw[i:i + 1, :], row_sb[:])
            p.bar()


def emit_attn1b_out(nc, p, ext):
    xs = ext["xs"]; pAs = ext["pAs"]; sel = ext["sel"]; wq = ext["wq"]
    bq = ext["bq"]; wout = ext["wout"]; bob = ext["bob"]; gam = ext["gam"]
    bet = ext["bet"]; ident = ext["ident"]; ys = ext["ys"]
    with ExitStack() as st:
        en = st.enter_context
        wq_sb = en(nc.sbuf_tensor("wq_sb", [128, 16 * 128], BF16))
        wo_sb = en(nc.sbuf_tensor("wo_sb", [128, 16 * D], BF16))
        bq_sb = en(nc.sbuf_tensor("bq_sb", [128, 16], F32))
        bob_sb = en(nc.sbuf_tensor("bob_sb", [128, D], F32))
        gam_sb = en(nc.sbuf_tensor("gam_sb", [128, D], F32))
        bet_sb = en(nc.sbuf_tensor("bet_sb", [128, D], F32))
        pas_sb = en(nc.sbuf_tensor("pas_sb", [64, D], F32))
        sel_sb = en(nc.sbuf_tensor("sel_sb", [64, 4], F32))
        arow = en(nc.sbuf_tensor("arow", [1, NTOK], F32))
        acol = en(nc.sbuf_tensor("acol", [128, 64], F32))
        ones1 = en(nc.sbuf_tensor("ones1", [1, 1], F32))
        idb_sb = en(nc.sbuf_tensor("idb_sb", [128, 128], BF16))
        xt = en(nc.sbuf_tensor("xt", [128, D], F32))
        stat = en(nc.sbuf_tensor("stat", [128, 8], F32))
        tmp1 = en(nc.sbuf_tensor("tmp1", [128, D], F32))
        tmp2 = en(nc.sbuf_tensor("tmp2", [128, D], F32))
        xln1 = en(nc.sbuf_tensor("xln1", [128, D], F32))
        xnb = en(nc.sbuf_tensor("xnb", [128, D], BF16))
        XnT = en(nc.sbuf_tensor("XnT", [128, 16 * 128], BF16))
        YT = en(nc.sbuf_tensor("YT", [128, 16 * 128], BF16))
        yt = en(nc.sbuf_tensor("yt", [128, 512], F32))
        ps0 = en(nc.psum_tensor("ps0", [128, 512], F32))
        ps1 = en(nc.psum_tensor("ps1", [128, 512], F32))
        pT = en(nc.psum_tensor("pT", [128, 1024], BF16))
        p.dma("gpsimd", wo_sb[:].rearrange("p (dt c) -> p dt c", c=D),
              wout.rearrange("(dt p) c -> p dt c", p=128))
        p.dma("sync", bq_sb[:], bq)
        p.dma("sync", bob_sb[:], bob)
        p.dma("sync", gam_sb[:], gam)
        p.dma("sync", bet_sb[:], bet)
        p.dma("sync", pas_sb[:], pAs)
        p.dma("sync", sel_sb[:], sel)
        p.dma("gpsimd", idb_sb[:], ident)
        p.op("vector", lambda e: e.memset(ones1[:], 1.0))
        p.bar()
        for b in range(4):
            for sl in range(4):
                p.op("tensor", lambda e, b=b, sl=sl:
                     e.matmul(ps0[0:1, 0:512], sel_sb[:, b:b + 1],
                              pas_sb[:, sl * 512:(sl + 1) * 512],
                              start=True, stop=True))
                p.bar()
                av = arow[0:1, b * 2048 + sl * 512:b * 2048 + (sl + 1) * 512]
                p.op("vector", lambda e, av=av:
                     e.tensor_scalar(av, ps0[0:1, 0:512], 0.5, 0.5,
                                     ALU.mult, ALU.add))
                p.bar()
        for b in range(4):
            for kt2 in range(16):
                av = arow[0:1, b * 2048 + kt2 * 128:b * 2048 + (kt2 + 1) * 128]
                p.op("tensor", lambda e, av=av:
                     e.matmul(ps1[:, 0:1], av, ones1[:], start=True, stop=True))
                p.bar()
                p.op("vector", lambda e, b=b, kt2=kt2:
                     e.tensor_copy(acol[:, b * 16 + kt2:b * 16 + kt2 + 1],
                                   ps1[:, 0:1]))
                p.bar()
        for i in range(8):
            b = i // 2
            xv = xt[:, 0:D]
            p.dma("sync", xt[:, 0:D], xs[i * 128:(i + 1) * 128, :])
            p.bar()
            emit_ln(p, xv, xln1[:], stat, tmp1, tmp2)
            emit_ln(p, xln1[:], xnb[:], stat, tmp1, tmp2, gam_sb[:], bet_sb[:])
            emit_transpose16(p, xnb[:],
                             XnT[:].rearrange("q (dt c) -> q dt c", c=128),
                             pT, idb_sb)
            for qt2 in range(16):
                p.dma("gpsimd", wq_sb[:].rearrange("p (dt c) -> p dt c", c=128),
                      wq[:, qt2 * 128:(qt2 + 1) * 128]
                      .rearrange("(dt p) c -> p dt c", p=128))
                p.bar()
                for dt in range(16):
                    p.op("tensor", lambda e, dt=dt:
                         e.matmul(ps0[:, 0:128],
                                  wq_sb[:, dt * 128:(dt + 1) * 128],
                                  XnT[:, dt * 128:(dt + 1) * 128],
                                  start=(dt == 0), stop=(dt == 15)))
                p.bar()
                abc = acol[:, b * 16 + qt2:b * 16 + qt2 + 1].broadcast_to((128, 128))
                p.op("vector", lambda e, qt2=qt2, abc=abc:
                     e.scalar_tensor_tensor(YT[:, qt2 * 128:(qt2 + 1) * 128],
                                            ps0[:, 0:128], bq_sb[:, qt2:qt2 + 1],
                                            abc, ALU.add, ALU.mult))
                p.bar()
            for ob in range(4):
                bank = (ps0, ps1)[ob % 2]
                for kt2 in range(16):
                    p.op("tensor", lambda e, kt2=kt2, ob=ob, bank=bank:
                         e.matmul(bank[:], YT[:, kt2 * 128:(kt2 + 1) * 128],
                                  wo_sb[:, kt2 * D + ob * 512:kt2 * D + (ob + 1) * 512],
                                  start=(kt2 == 0), stop=(kt2 == 15)))
                p.bar()
                p.op("vector", lambda e, ob=ob, bank=bank:
                     e.tensor_tensor(yt[:], bank[:],
                                     bob_sb[:, ob * 512:(ob + 1) * 512], ALU.add))
                p.bar()
                p.op("vector", lambda e, ob=ob:
                     e.tensor_tensor(yt[:], yt[:],
                                     xln1[:, ob * 512:(ob + 1) * 512], ALU.add))
                p.bar()
                xv2 = xt[:, ob * 512:(ob + 1) * 512]
                p.op("vector", lambda e, xv2=xv2:
                     e.tensor_tensor(yt[:], yt[:], xv2, ALU.add))
                p.bar()
                p.dma("sync", ys[i * 128:(i + 1) * 128, ob * 512:(ob + 1) * 512],
                      yt[:])
            p.bar()


def build_attn1b_partial():
    nc = bass.Bass("TRN2", target_bir_lowering=False)
    ext = dict(
        xs=nc.dram_tensor("xs", [RPC, D], F32, kind="ExternalInput").ap(),
        wkv=nc.dram_tensor("wkv", [D, 4096], F32, kind="ExternalInput").ap(),
        bkv=nc.dram_tensor("bkv", [128, 4096], F32, kind="ExternalInput").ap(),
        gam=nc.dram_tensor("gam", [128, D], F32, kind="ExternalInput").ap(),
        bet=nc.dram_tensor("bet", [128, D], F32, kind="ExternalInput").ap(),
        coss=nc.dram_tensor("coss", [128, 1024], F32, kind="ExternalInput").ap(),
        sins=nc.dram_tensor("sins", [128, 1024], F32, kind="ExternalInput").ap(),
        ident=nc.dram_tensor("ident", [128, 128], F32, kind="ExternalInput").ap(),
        praw=nc.dram_tensor("praw", [8, D], F32, kind="ExternalOutput").ap())
    p = Prog(nc)
    emit_attn1b_partial(nc, p, ext)
    p.emit()
    return nc


def build_attn1b_out():
    nc = bass.Bass("TRN2", target_bir_lowering=False)
    ext = dict(
        xs=nc.dram_tensor("xs", [RPC, D], F32, kind="ExternalInput").ap(),
        pAs=nc.dram_tensor("pAs", [64, D], F32, kind="ExternalInput").ap(),
        sel=nc.dram_tensor("sel", [64, 4], F32, kind="ExternalInput").ap(),
        wq=nc.dram_tensor("wq", [D, D], F32, kind="ExternalInput").ap(),
        bq=nc.dram_tensor("bq", [128, 16], F32, kind="ExternalInput").ap(),
        wout=nc.dram_tensor("wout", [D, D], F32, kind="ExternalInput").ap(),
        bob=nc.dram_tensor("bob", [128, D], F32, kind="ExternalInput").ap(),
        gam=nc.dram_tensor("gam", [128, D], F32, kind="ExternalInput").ap(),
        bet=nc.dram_tensor("bet", [128, D], F32, kind="ExternalInput").ap(),
        ident=nc.dram_tensor("ident", [128, 128], F32, kind="ExternalInput").ap(),
        ys=nc.dram_tensor("ys", [RPC, D], F32, kind="ExternalOutput").ap())
    p = Prog(nc)
    emit_attn1b_out(nc, p, ext)
    p.emit()
    return nc


class _NCAlias:

    def __init__(self, nc, pref):
        self._nc, self._pref = nc, pref

    def sbuf_tensor(self, name, *a, **kw):
        return self._nc.sbuf_tensor(self._pref + name, *a, **kw)

    def psum_tensor(self, name, *a, **kw):
        return self._nc.psum_tensor(self._pref + name, *a, **kw)

    def __getattr__(self, k):
        return getattr(self._nc, k)


def emit_lnT_shard(nc, p, xsrc, agin, ident):
    with ExitStack() as st:
        en = st.enter_context
        xt = en(nc.sbuf_tensor("lsh_xt", [128, D], F32))
        stat = en(nc.sbuf_tensor("lsh_stat", [128, 8], F32))
        tmp1 = en(nc.sbuf_tensor("lsh_tmp1", [128, D], F32))
        tmp2 = en(nc.sbuf_tensor("lsh_tmp2", [128, D], F32))
        xlnb = en(nc.sbuf_tensor("lsh_xlnb", [128, D], BF16))
        tst = en(nc.sbuf_tensor("lsh_tst", [128, 16 * 128], BF16))
        idb = en(nc.sbuf_tensor("lsh_idb", [128, 128], BF16))
        pT = en(nc.psum_tensor("lsh_pT", [128, 1024], BF16))
        p.dma("gpsimd", idb[:], ident)
        p.bar()
        for i in range(8):
            p.dma("sync", xt[:], xsrc[i * 128:(i + 1) * 128, :])
            p.bar()
            emit_ln(p, xt[:], xlnb[:], stat, tmp1, tmp2)
            emit_transpose16(p, xlnb[:],
                             tst[:].rearrange("q (dt c) -> q dt c", c=128),
                             pT, idb)
            p.dma("sync",
                  agin[:, i * 128:(i + 1) * 128].rearrange(
                      "(dt q) c -> q dt c", q=128),
                  tst[:].rearrange("q (dt c) -> q dt c", c=128))
            p.bar()


def build_fused():
    from concourse.aot_env import allow_ici_all_to_all
    RG = [list(range(NCORE))]
    nc = bass.Bass("TRN2", target_bir_lowering=False)

    def inp(name, shp, dt=F32):
        return nc.dram_tensor(name, shp, dt, kind="ExternalInput").ap()

    xs = inp("xs", [RPC, D])
    ident = inp("ident", [128, 128])
    ea = dict(x=None, wmix=inp("a_wmix", [D, 1024]), wz=inp("a_wz", [D, 512]),
              wba=inp("a_wba", [D, 8]), convw=inp("a_convw", [128, 32]),
              dtb=inp("a_dtb", [128, 4]), alog=inp("a_alog", [128, 4]),
              normw=inp("a_normw", [128, 1]), ident=ident,
              trii=inp("trii", [128, 128]), maskl=inp("a_maskl", [128, 512]),
              masku=inp("a_masku", [128, 512]), identv=inp("a_identv", [128, 512]))
    wout_a = inp("a_wout", [4096, D])
    eb = dict(x=None, wmix=inp("b_wmix", [D, 768]), wz=inp("b_wz", [D, 256]),
              wba=inp("b_wba", [D, 4]), convw=inp("b_convw", [128, 24]),
              dtb=inp("b_dtb", [128, 2]), alog=inp("b_alog", [128, 2]),
              normw=inp("b_normw", [128, 1]), ident=ident,
              trii=ea["trii"], maskl=inp("b_maskl", [128, 256]),
              masku=inp("b_masku", [128, 256]), identv=inp("b_identv", [128, 256]))
    wout_b = inp("b_wout", [2048, D])
    ef = dict(gwT=inp("f_gwT", [D, 5632]), uwT=inp("f_uwT", [D, 5632]),
              dwT=inp("f_dwT", [5632, D]), gsc=inp("f_gsc", [128, 44]),
              usc=inp("f_usc", [128, 44]), dsb=inp("f_dsb", [128, D]),
              ident=ident)
    e1a = dict(wqkvT=inp("q_wqkvT", [D, 3072]), wprojT=inp("q_wprojT", [D, D]),
               ident=ident)
    em = dict(wg=inp("m_wg", [8, D, 1408]), wu=inp("m_wu", [8, D, 1408]),
              wd=inp("m_wd", [8, 1408, D]), sg=inp("m_sg", [D, 5632]),
              su=inp("m_su", [D, 5632]), sd=inp("m_sd", [5632, D]),
              wr=inp("m_wr", [D, 8]), wsg=inp("m_wsg", [128, 16]), ident=ident)
    ep1 = dict(wkv=inp("p_wkv", [D, 4096]), bkv=inp("p_bkv", [128, 4096]),
               gam=inp("p_gam", [128, D]), bet=inp("p_bet", [128, D]),
               coss=inp("p_coss", [128, 1024]), sins=inp("p_sins", [128, 1024]),
               ident=ident)
    eo1 = dict(sel=inp("o_sel", [64, 4]), wq=inp("o_wq", [D, D]),
               bq=inp("o_bq", [128, 16]), wout=inp("o_wout", [D, D]),
               bob=inp("o_bob", [128, D]), gam=ep1["gam"], bet=ep1["bet"],
               ident=ident)
    ys_ext = nc.dram_tensor("ys", [RPC, D], F32, kind="ExternalOutput").ap()

    xd = nc.dram_tensor("xd", [RPC, D], F32).ap()
    aga_in = nc.dram_tensor("aga_in", [D, RPC], BF16).ap()
    aga_out = nc.dram_tensor("aga_out", [8 * D, RPC], BF16).ap()
    a2aa_in = nc.dram_tensor("a2aa_in", [8 * 512, RPC], BF16).ap()
    a2aa_out = nc.dram_tensor("a2aa_out", [8 * 512, RPC], BF16).ap()
    a2ab_in = nc.dram_tensor("a2ab_in", [8 * 256, RPC], BF16).ap()
    a2ab_out = nc.dram_tensor("a2ab_out", [8 * 256, RPC], BF16).ap()
    praw = nc.dram_tensor("praw", [8, D], F32).ap()
    praw_all = nc.dram_tensor("praw_all", [64, D], F32).ap()

    p = Prog(nc)

    def ag(in_ap, out_ap):
        p.cc(lambda e: e.collective_compute(
            "AllGather", ALU.bypass, replica_groups=RG,
            ins=[in_ap.opt()], outs=[out_ap.opt()]))

    def a2a(in_ap, out_ap):
        def f(e):
            with allow_ici_all_to_all():
                return e.collective_compute(
                    "AllToAll", ALU.bypass, replica_groups=RG,
                    ins=[in_ap.opt()], outs=[out_ap.opt()])
        p.cc(f)

    emit_lnT_shard(_NCAlias(nc, "s1l_"), p, xs, aga_in, ident)
    ag(aga_in, aga_out)
    ea2 = dict(ea); ea2["xlnT"] = aga_out; ea2["gout"] = a2aa_in
    emit_deltanet(_NCAlias(nc, "s1d_"), p, ea2, 2, 2, "mul_first", fused=True)
    a2a(a2aa_in, a2aa_out)
    emit_matout(_NCAlias(nc, "s1m_"), p, dict(xs=xs, gts=a2aa_out, wout=wout_a, ys=xd), 4096)
    emit_ffn0(_NCAlias(nc, "s2_"), p, dict(ef, xs=xd, ys=xd))
    emit_lnT_shard(_NCAlias(nc, "s3l_"), p, xd, aga_in, ident)
    ag(aga_in, aga_out)
    eb2 = dict(eb); eb2["xlnT"] = aga_out; eb2["gout"] = a2ab_in
    emit_deltanet(_NCAlias(nc, "s3d_"), p, eb2, 2, 1, "norm_first", fused=True)
    a2a(a2ab_in, a2ab_out)
    emit_matout(_NCAlias(nc, "s3m_"), p, dict(xs=xd, gts=a2ab_out, wout=wout_b, ys=xd), 2048)
    emit_ffn0(_NCAlias(nc, "s4_"), p, dict(ef, xs=xd, ys=xd))
    emit_attn1a(_NCAlias(nc, "s5_"), p, dict(e1a, xs=xd, ys=xd))
    emit_moe(_NCAlias(nc, "s6_"), p, dict(em, xs=xd, ys=xd))
    emit_attn1b_partial(_NCAlias(nc, "s7p_"), p, dict(ep1, xs=xd, praw=praw))
    ag(praw, praw_all)
    emit_attn1b_out(_NCAlias(nc, "s7o_"), p, dict(eo1, xs=xd, pAs=praw_all, ys=xd))
    emit_moe(_NCAlias(nc, "s8_"), p, dict(em, xs=xd, ys=ys_ext))

    p.emit()
    return nc


def _fused_maps(ins, x):
    xsh = _shard(x)
    ident = np.eye(128, dtype=np.float32)
    gwT = np.ascontiguousarray(np.asarray(ins["L0_ffn_gate_w"], np.float32).T)
    uwT = np.ascontiguousarray(np.asarray(ins["L0_ffn_up_w"], np.float32).T)
    dwT = np.ascontiguousarray(np.asarray(ins["L0_ffn_down_w"], np.float32).T)
    gsc = np.ascontiguousarray(
        np.asarray(ins["L0_ffn_gate_s"], np.float32).reshape(44, 128).T)
    usc = np.ascontiguousarray(
        np.asarray(ins["L0_ffn_up_s"], np.float32).reshape(44, 128).T)
    dsb = np.tile(np.asarray(ins["L0_ffn_down_s"], np.float32)[None, :], (128, 1))
    W_in = np.asarray(ins["L1_attn_b_W_in"], np.float32)
    b_in = np.asarray(ins["L1_attn_b_b_in"], np.float32)
    gam = np.tile(np.asarray(ins["L1_attn_b_gamma"], np.float32)[None, :], (128, 1))
    bet = np.tile(np.asarray(ins["L1_attn_b_beta"], np.float32)[None, :], (128, 1))
    selm = np.zeros((64, 4), np.float32)
    for r in range(64):
        selm[r, (r % 8) // 2] = 1.0
    ROT = D // 2
    inv = 1.0 / (10000.0 ** (np.arange(0, ROT, 2, dtype=np.float32) / ROT))
    shared = dict(
        ident=ident,
        a_wout=np.ascontiguousarray(np.asarray(ins["L0_attn_a_W_out"], np.float32)),
        b_wout=np.ascontiguousarray(np.asarray(ins["L0_attn_b_W_out"], np.float32)),
        f_gwT=gwT, f_uwT=uwT, f_dwT=dwT, f_gsc=gsc, f_usc=usc, f_dsb=dsb,
        q_wqkvT=np.ascontiguousarray(
            np.asarray(ins["L1_attn_a_weight_qkv"], np.float32).T),
        q_wprojT=np.ascontiguousarray(
            np.asarray(ins["L1_attn_a_weight_proj"], np.float32).T),
        m_wg=np.ascontiguousarray(np.asarray(ins["L1_ffn_Wg"], np.float32)),
        m_wu=np.ascontiguousarray(np.asarray(ins["L1_ffn_Wu"], np.float32)),
        m_wd=np.ascontiguousarray(np.asarray(ins["L1_ffn_Wd"], np.float32)),
        m_sg=np.ascontiguousarray(np.asarray(ins["L1_ffn_Sg"], np.float32)),
        m_su=np.ascontiguousarray(np.asarray(ins["L1_ffn_Su"], np.float32)),
        m_sd=np.ascontiguousarray(np.asarray(ins["L1_ffn_Sd"], np.float32)),
        m_wr=np.ascontiguousarray(np.asarray(ins["L1_ffn_Wr"], np.float32)),
        m_wsg=np.ascontiguousarray(
            np.asarray(ins["L1_ffn_Wsg"], np.float32).reshape(16, 128).T),
        p_wkv=np.ascontiguousarray(W_in[:, 2048:6144]),
        p_bkv=np.tile(b_in[None, 2048:6144], (128, 1)),
        p_gam=gam, p_bet=bet,
        o_sel=selm, o_wq=np.ascontiguousarray(W_in[:, 0:2048]),
        o_bq=np.ascontiguousarray(b_in[0:2048].reshape(16, 128).T),
        o_wout=np.ascontiguousarray(
            np.asarray(ins["L1_attn_b_W_out"], np.float32)),
        o_bob=np.tile(np.asarray(ins["L1_attn_b_b_out"], np.float32)[None, :],
                      (128, 1)),
    )
    maps = []
    for c in range(NCORE):
        da = prep_deltanet_a(ins, c, x)
        db = prep_deltanet_b(ins, c, x)
        ts0 = np.arange(256 * c, 256 * c + 128, dtype=np.float32)
        ts1 = np.arange(256 * c + 128, 256 * c + 256, dtype=np.float32)
        cs, sn = [], []
        for tsv in (ts0, ts1):
            fr = tsv[:, None] * inv[None, :]
            cs.append(np.cos(fr).astype(np.float32))
            sn.append(np.sin(fr).astype(np.float32))
        m = dict(shared)
        m.update(xs=xsh[c], trii=da["trii"],
                 a_wmix=da["wmix"], a_wz=da["wz"], a_wba=da["wba"],
                 a_convw=da["convw"], a_dtb=da["dtb"], a_alog=da["alog"],
                 a_normw=da["normw"], a_maskl=da["maskl"], a_masku=da["masku"],
                 a_identv=da["identv"],
                 b_wmix=db["wmix"], b_wz=db["wz"], b_wba=db["wba"],
                 b_convw=db["convw"], b_dtb=db["dtb"], b_alog=db["alog"],
                 b_normw=db["normw"], b_maskl=db["maskl"], b_masku=db["masku"],
                 b_identv=db["identv"],
                 p_coss=np.ascontiguousarray(np.concatenate(cs, 1)),
                 p_sins=np.ascontiguousarray(np.concatenate(sn, 1)))
        maps.append(m)
    return maps


NSUB = 8
_prog_cache = {}


def _get_prog(key, builder, *args):
    if key not in _prog_cache:
        _prog_cache[key] = builder(*args)
    return _prog_cache[key]


def _shard(xf):
    xr = xf.reshape(4, 8, 256, D)
    return [np.ascontiguousarray(xr[:, c].reshape(RPC, D)) for c in range(NCORE)]


def _unshard(ys):
    return (np.stack(ys, 0).reshape(8, 4, 256, D).transpose(1, 0, 2, 3)
            .reshape(NTOK, D))


def _run(nc, in_maps, outkey):
    res = run_bass_kernel_spmd(nc, in_maps, core_ids=list(range(NCORE)))
    return [np.asarray(r[outkey]) for r in res.results]


def _deltanet_sub(ins, x, which):
    if which == "a":
        nc = _get_prog("dn_a", build_deltanet, 2, 2, "mul_first")
        maps = [prep_deltanet_a(ins, c, x) for c in range(NCORE)]
        KV = 4096
        wout = np.ascontiguousarray(np.asarray(ins["L0_attn_a_W_out"], np.float32))
    else:
        nc = _get_prog("dn_b", build_deltanet, 2, 1, "norm_first")
        maps = [prep_deltanet_b(ins, c, x) for c in range(NCORE)]
        KV = 2048
        wout = np.ascontiguousarray(np.asarray(ins["L0_attn_b_W_out"], np.float32))
    gts = _run(nc, maps, "gatedT")
    gt = np.concatenate(gts, axis=0)
    gtr = gt.reshape(KV, 4, 8, 256)
    nc2 = _get_prog("mo%d" % KV, build_matout, KV)
    xsh = _shard(x)
    maps2 = [dict(xs=xsh[c],
                  gts=np.ascontiguousarray(gtr[:, :, c, :].reshape(KV, RPC)),
                  wout=wout) for c in range(NCORE)]
    return _unshard(_run(nc2, maps2, "ys"))


def _ffn0_sub(ins, x):
    nc = _get_prog("ffn0", build_ffn0)
    gwT = np.ascontiguousarray(np.asarray(ins["L0_ffn_gate_w"], np.float32).T)
    uwT = np.ascontiguousarray(np.asarray(ins["L0_ffn_up_w"], np.float32).T)
    dwT = np.ascontiguousarray(np.asarray(ins["L0_ffn_down_w"], np.float32).T)
    gsc = np.ascontiguousarray(
        np.asarray(ins["L0_ffn_gate_s"], np.float32).reshape(44, 128).T)
    usc = np.ascontiguousarray(
        np.asarray(ins["L0_ffn_up_s"], np.float32).reshape(44, 128).T)
    dsb = np.tile(np.asarray(ins["L0_ffn_down_s"], np.float32)[None, :], (128, 1))
    ident = np.eye(128, dtype=np.float32)
    xsh = _shard(x)
    maps = [dict(xs=xsh[c], gwT=gwT, uwT=uwT, dwT=dwT, gsc=gsc, usc=usc,
                 dsb=dsb, ident=ident) for c in range(NCORE)]
    return _unshard(_run(nc, maps, "ys"))


def _attn1a_sub(ins, x):
    nc = _get_prog("a1a", build_attn1a)
    wqkvT = np.ascontiguousarray(
        np.asarray(ins["L1_attn_a_weight_qkv"], np.float32).T)
    wprojT = np.ascontiguousarray(
        np.asarray(ins["L1_attn_a_weight_proj"], np.float32).T)
    ident = np.eye(128, dtype=np.float32)
    xsh = _shard(x)
    maps = [dict(xs=xsh[c], wqkvT=wqkvT, wprojT=wprojT, ident=ident)
            for c in range(NCORE)]
    return _unshard(_run(nc, maps, "ys"))


def _moe_sub(ins, x):
    nc = _get_prog("moe", build_moe)
    args = dict(
        wg=np.ascontiguousarray(np.asarray(ins["L1_ffn_Wg"], np.float32)),
        wu=np.ascontiguousarray(np.asarray(ins["L1_ffn_Wu"], np.float32)),
        wd=np.ascontiguousarray(np.asarray(ins["L1_ffn_Wd"], np.float32)),
        sg=np.ascontiguousarray(np.asarray(ins["L1_ffn_Sg"], np.float32)),
        su=np.ascontiguousarray(np.asarray(ins["L1_ffn_Su"], np.float32)),
        sd=np.ascontiguousarray(np.asarray(ins["L1_ffn_Sd"], np.float32)),
        wr=np.ascontiguousarray(np.asarray(ins["L1_ffn_Wr"], np.float32)),
        wsg=np.ascontiguousarray(
            np.asarray(ins["L1_ffn_Wsg"], np.float32).reshape(16, 128).T),
        ident=np.eye(128, dtype=np.float32))
    xsh = _shard(x)
    maps = [dict(xs=xsh[c], **args) for c in range(NCORE)]
    return _unshard(_run(nc, maps, "ys"))


def _attn1b_sub(ins, x):
    W_in = np.asarray(ins["L1_attn_b_W_in"], np.float32)
    b_in = np.asarray(ins["L1_attn_b_b_in"], np.float32)
    gam = np.tile(np.asarray(ins["L1_attn_b_gamma"], np.float32)[None, :], (128, 1))
    bet = np.tile(np.asarray(ins["L1_attn_b_beta"], np.float32)[None, :], (128, 1))
    ident = np.eye(128, dtype=np.float32)
    xsh = _shard(x)
    ROT = D // 2
    inv = 1.0 / (10000.0 ** (np.arange(0, ROT, 2, dtype=np.float32) / ROT))
    nc1 = _get_prog("a1b_p", build_attn1b_partial)
    maps1 = []
    for c in range(NCORE):
        ts0 = np.arange(256 * c, 256 * c + 128, dtype=np.float32)
        ts1 = np.arange(256 * c + 128, 256 * c + 256, dtype=np.float32)
        cs, sn = [], []
        for tsv in (ts0, ts1):
            fr = tsv[:, None] * inv[None, :]
            cs.append(np.cos(fr).astype(np.float32))
            sn.append(np.sin(fr).astype(np.float32))
        maps1.append(dict(
            xs=xsh[c],
            wkv=np.ascontiguousarray(W_in[:, 2048:6144]),
            bkv=np.tile(b_in[None, 2048:6144], (128, 1)),
            gam=gam, bet=bet,
            coss=np.ascontiguousarray(np.concatenate(cs, 1)),
            sins=np.ascontiguousarray(np.concatenate(sn, 1)),
            ident=ident))
    pas = _run(nc1, maps1, "praw")
    pas_all = np.ascontiguousarray(np.concatenate(pas, axis=0))
    selm = np.zeros((64, 4), np.float32)
    for r in range(64):
        selm[r, (r % 8) // 2] = 1.0
    nc2 = _get_prog("a1b_o", build_attn1b_out)
    maps2 = [dict(xs=xsh[c], pAs=pas_all, sel=selm,
                  wq=np.ascontiguousarray(W_in[:, 0:2048]),
                  bq=np.ascontiguousarray(b_in[0:2048].reshape(16, 128).T),
                  wout=np.ascontiguousarray(
                      np.asarray(ins["L1_attn_b_W_out"], np.float32)),
                  bob=np.tile(np.asarray(ins["L1_attn_b_b_out"],
                                         np.float32)[None, :], (128, 1)),
                  gam=gam, bet=bet, ident=ident) for c in range(NCORE)]
    return _unshard(_run(nc2, maps2, "ys"))


FUSED = True


def kernel(**ins):
    x = np.ascontiguousarray(
        np.asarray(ins["L0_attn_a_hidden_states"], np.float32).reshape(NTOK, D))
    if FUSED:
        from concourse.aot_env import allow_ici_all_to_all
        nc = _get_prog("fused", build_fused)
        maps = _fused_maps(ins, x)
        with allow_ici_all_to_all():
            res = run_bass_kernel_spmd(nc, maps, core_ids=list(range(NCORE)))
        ys = [np.asarray(r["ys"]) for r in res.results]
        return _unshard(ys).reshape(4, 2048, 2048)
    steps = [
        lambda x: _deltanet_sub(ins, x, "a"),
        lambda x: _ffn0_sub(ins, x),
        lambda x: _deltanet_sub(ins, x, "b"),
        lambda x: _ffn0_sub(ins, x),
        lambda x: _attn1a_sub(ins, x),
        lambda x: _moe_sub(ins, x),
        lambda x: _attn1b_sub(ins, x),
        lambda x: _moe_sub(ins, x),
    ]
    import sys
    for k in range(NSUB):
        x = steps[k](x)
        print(f"[kernel] sublayer {k + 1}/{NSUB} done", file=sys.stderr)
    return x.reshape(4, 2048, 2048)
